# version 13
# baseline (speedup 1.0000x reference)
import contextlib
import math
from dataclasses import dataclass

import numpy as np
import ml_dtypes

import concourse.bass as bass
import concourse.mybir as mybir
from concourse.bass_utils import run_bass_kernel_spmd

F32 = mybir.dt.float32
BF16 = mybir.dt.bfloat16
AF = mybir.ActivationFunctionType
ALU = mybir.AluOpType
AX = mybir.AxisListType

ENGINES = ("sync", "tensor", "vector", "scalar", "gpsimd")


@dataclass
class Cfg:
    S: int = 8192
    D: int = 2048
    DFF: int = 1408
    NH: int = 16
    HPC: int = 2
    DH: int = 128
    NS: int = 2
    NR: int = 16
    KR: int = 4
    R: int = 8

    @property
    def DC(self):  return self.D // 128
    @property
    def FFC(self): return self.DFF // 128
    @property
    def NE(self):  return self.NS + self.NR
    @property
    def TOK(self): return self.S // self.R
    @property
    def TT(self):  return self.TOK // 128
    @property
    def NT(self):  return self.S // 128
    @property
    def NKT(self): return self.S // 128
    @property
    def QC(self):  return self.S // 512
    @property
    def SLABS(self): return self.NT // 8
    @property
    def ND256(self): return self.D // 256
    @property
    def DB512(self): return self.D // 512
    @property
    def TCO(self): return self.TOK // 512


class Sem:
    def __init__(self, stack, nc, name):
        self.h = stack.enter_context(nc.semaphore(name))
        self.n = 0
    def take(self, k):
        self.n += k
        return self.n


class Phase:
    def __init__(self):
        self.q = {e: [] for e in ENGINES}
    def do(self, e, fn):
        self.q[e].append(fn)


def flush(block, ph):
    for e in ENGINES:
        fns = ph.q[e]
        if fns:
            getattr(block, e)(lambda E, fns=fns: [f(E) for f in fns])


def build(cfg: Cfg):
    c = cfg
    assert c.KR == 4 and c.DH == 128 and c.D % 512 == 0 and c.TOK % 512 == 0
    assert c.NT % 8 == 0 and c.S % 512 == 0
    ISQ = 1.0 / math.sqrt(float(c.DH))
    ND512 = c.D // 512
    PER = min(8, c.DC)
    NG = max(1, c.DC // PER)

    nc = bass.Bass("TRN2", target_bir_lowering=False)

    xall = nc.dram_tensor("xall", [c.S, c.D], F32, kind="ExternalInput").ap()
    xown = nc.dram_tensor("xown", [c.TOK, c.D], F32, kind="ExternalInput").ap()
    wqkvT = nc.dram_tensor("wqkvT", [3 * c.HPC, 128, c.D], F32, kind="ExternalInput").ap()
    wotT = nc.dram_tensor("wotT", [c.HPC, 128, c.D], F32, kind="ExternalInput").ap()
    w1R = nc.dram_tensor("w1R", [c.NE, c.FFC, 128, c.D], F32, kind="ExternalInput").ap()
    w2R = nc.dram_tensor("w2R", [c.NE, c.DB512, 128, c.FFC * 512], F32, kind="ExternalInput").ap()
    b1R = nc.dram_tensor("b1R", [128, c.NE * c.FFC], F32, kind="ExternalInput").ap()
    b2R = nc.dram_tensor("b2R", [c.NE, c.D], F32, kind="ExternalInput").ap()
    centT = nc.dram_tensor("centT", [128, c.DC * c.NR], F32, kind="ExternalInput").ap()
    biasr = nc.dram_tensor("biasr", [1, c.NR], F32, kind="ExternalInput").ap()
    yout = nc.dram_tensor("y", [c.TOK, c.D], F32, kind="ExternalOutput").ap()
    sp_d = nc.dram_tensor("s_partial", [c.S, c.D], BF16)
    z_d = nc.dram_tensor("z_d", [128, c.HPC * c.S], BF16)
    so_d = nc.dram_tensor("s_own", [c.TOK, c.D], BF16)

    stack = contextlib.ExitStack()
    with stack:
        DX = [Sem(stack, nc, f"dx{i}") for i in range(4)]
        DW = Sem(stack, nc, "dw")
        DF = [Sem(stack, nc, f"df{i}") for i in range(2)]
        DMG = Sem(stack, nc, "dmg")
        DS = [Sem(stack, nc, f"ds{i}") for i in range(2)]
        DY = [Sem(stack, nc, f"dy{i}") for i in range(2)]
        PE = Sem(stack, nc, "pe")
        DVE = Sem(stack, nc, "dve")
        ACT = Sem(stack, nc, "act")
        POOL = Sem(stack, nc, "pool")
        CC = Sem(stack, nc, "cc")
        RSD = Sem(stack, nc, "rsd")
        BAR = Sem(stack, nc, "bar")
        ALL_DMA = {"sync": [DX[0], DX[1], DX[2], DX[3], DW, DF[0], DF[1], DY[0], DY[1]],
                   "gpsimd": [DMG], "scalar": [DS[0], DS[1]]}

        SERIAL = {"vector": DVE, "scalar": ACT, "gpsimd": POOL}

        def W(E, waits):
            for (h, v) in waits:
                if v > 0:
                    E.wait_ge(h, v)

        def dma(ph, q, sem, dst, src, waits=()):
            waits = list(waits)
            if q in SERIAL:
                s = SERIAL[q]
                waits.append((s.h, s.n))
            waits.append((sem.h, sem.n))
            t = sem.take(16)
            def f(E, dst=dst, src=src, sem=sem, waits=tuple(waits)):
                W(E, waits)
                E.dma_start(out=dst, in_=src).then_inc(sem.h, 16)
            ph.do(q, f)
            return t

        def comp(ph, eng, sem, fn, waits=()):
            waits = list(waits)
            if eng in SERIAL:
                s = SERIAL[eng]
                waits.append((s.h, s.n))
            t = sem.take(1)
            def f(E, fn=fn, sem=sem, waits=tuple(waits)):
                W(E, waits)
                fn(E).then_inc(sem.h, 1)
            ph.do(eng, f)
            return t

        def compn(ph, eng, fn, waits=()):
            if eng in SERIAL:
                return comp(ph, eng, SERIAL[eng], fn, waits)
            def f(E, fn=fn, waits=tuple(waits)):
                W(E, waits)
                fn(E)
            ph.do(eng, f)
            return 0

        def barrier(ph):
            for q, sems in ALL_DMA.items():
                for s in sems:
                    ph.do(q, lambda E, s=s, v=s.n: E.wait_ge(s.h, v) if v else None)
            for e, s in (("vector", DVE), ("scalar", ACT), ("gpsimd", POOL), ("tensor", PE)):
                ph.do(e, lambda E, s=s, v=s.n: E.wait_ge(s.h, v) if v else None)
            for e in ENGINES:
                BAR.take(1)
                ph.do(e, lambda E: E.sem_inc(BAR.h, 1))
            tgt = BAR.n
            for e in ENGINES:
                ph.do(e, lambda E, tgt=tgt: E.wait_ge(BAR.h, tgt))

        S2a = stack.enter_context(nc.psum_tensor("S2a", [128, 1024], F32))
        S2b = stack.enter_context(nc.psum_tensor("S2b", [128, 1024], F32))
        Zb = stack.enter_context(nc.psum_tensor("Zb", [128, 512], F32))
        SU = stack.enter_context(nc.psum_tensor("SU", [128, 512], F32))
        W1b = stack.enter_context(nc.psum_tensor("W1b", [128, 512], F32))
        W2b = stack.enter_context(nc.psum_tensor("W2b", [128, 512], F32))
        S2 = [S2a, S2b]
        WOp = [W1b, W2b]

        identF = stack.enter_context(nc.sbuf_tensor("identF", [128, 128], F32))
        ones128b = stack.enter_context(nc.sbuf_tensor("ones128b", [128, 1], BF16))
        ones1f = stack.enter_context(nc.sbuf_tensor("ones1f", [1, 128], F32))
        epsT = stack.enter_context(nc.sbuf_tensor("epsT", [128, 1], F32))
        biasbc = stack.enter_context(nc.sbuf_tensor("biasbc", [128, c.NR], F32))
        mv = [stack.enter_context(nc.sbuf_tensor(f"mv{i}", [128, 2], F32)) for i in range(2)]
        rstd = [stack.enter_context(nc.sbuf_tensor(f"rstd{i}", [128, 1], F32)) for i in range(2)]
        bn6 = stack.enter_context(nc.sbuf_tensor("bn6", [128, 6 * ND512], F32))

        with nc.Block() as block:
            s_qkv = contextlib.ExitStack()
            QTs = s_qkv.enter_context(nc.sbuf_tensor("QTs", [128, c.HPC * c.S], BF16))
            KTs = s_qkv.enter_context(nc.sbuf_tensor("KTs", [128, c.HPC * c.S], BF16))
            VTs = s_qkv.enter_context(nc.sbuf_tensor("VTs", [128, c.HPC * c.S], BF16))

            s_p1 = contextlib.ExitStack()
            xnTs = s_p1.enter_context(nc.sbuf_tensor("xnTs", [128, c.DC * 1024], BF16))
            wqkv = s_p1.enter_context(nc.sbuf_tensor("wqkv", [128, 3 * c.HPC * c.D], BF16))
            xt = [s_p1.enter_context(nc.sbuf_tensor(f"xt{i}", [128, c.D], F32)) for i in range(4)]
            xnf = [s_p1.enter_context(nc.sbuf_tensor(f"xnf{i}", [128, c.D], F32)) for i in range(2)]
            vstg = [s_p1.enter_context(nc.sbuf_tensor(f"vstg{i}", [128, 512], F32)) for i in range(2)]

            ph = Phase()
            compn(ph, "gpsimd", lambda E: E.memset(identF[:], 0.0))
            compn(ph, "gpsimd", lambda E: E.affine_select(
                out=identF[:], in_=identF[:], compare_op=ALU.not_equal, fill=1.0,
                base=0, pattern=[[-1, 128]], channel_multiplier=1))
            compn(ph, "gpsimd", lambda E: E.memset(ones128b[:], 1.0))
            compn(ph, "gpsimd", lambda E: E.memset(ones1f[:], 1.0))
            compn(ph, "gpsimd", lambda E: E.memset(epsT[:], 1e-5))
            t_consts = comp(ph, "gpsimd", POOL, lambda E: E.engine_nop())

            t_cast = {}
            for idx in range(3 * c.HPC):
                b = idx % 4
                wl = [(POOL.h, t_cast.get(idx - 4, 0))]
                dma(ph, "sync", DX[b], xt[b][:], wqkvT[idx], waits=wl)
                t_cast[idx] = comp(
                    ph, "gpsimd", POOL,
                    lambda E, idx=idx, b=b: E.tensor_copy(wqkv[:, idx * c.D:(idx + 1) * c.D], xt[b][:]),
                    waits=[(DX[b].h, DX[b].n), (POOL.h, t_consts)])
            pool_p0 = POOL.n
            dma(ph, "sync", DW, xt[0][0:1, 0:c.NR], biasr[:, :],
                waits=[(POOL.h, POOL.n)])
            tpe0 = comp(ph, "tensor", PE,
                        lambda E: E.matmul(W1b[:, 0:c.NR], ones1f[:], xt[0][0:1, 0:c.NR], start=True, stop=True),
                        waits=[(DW.h, DW.n), (POOL.h, pool_p0)])
            comp(ph, "vector", DVE, lambda E: E.tensor_copy(biasbc[:], W1b[:, 0:c.NR]),
                 waits=[(PE.h, tpe0)])

            t_xn = {}
            t_rstd = {}
            t_ev = {}
            t_trlast = {}
            s2war = {0: 0, 1: 0}
            s2war_eng = {0: ACT.h, 1: ACT.h}
            qkwar = {}
            t_aggr = {}

            def p1_A(t):
                b = t % 4
                wl = [(DVE.h, t_xn.get(t - 4, 0))]
                if t < 4:
                    wl += [(POOL.h, pool_p0), (PE.h, tpe0)]
                dma(ph, "sync", DX[b], xt[b][:], xall[t * 128:(t + 1) * 128, :], waits=wl)
                for i in range(ND512):
                    compn(ph, "vector",
                          lambda E, b=b, i=i: E.bn_stats(bn6[:, 6 * i:6 * (i + 1)], xt[b][:, 512 * i:512 * (i + 1)]),
                          waits=[(DX[b].h, DX[b].n)])
                t_aggr[t] = comp(ph, "vector", DVE, lambda E, bm=t % 2: E.bn_aggr(mv[bm][:], bn6[:]),
                                 waits=[(ACT.h, t_rstd.get(t - 2, 0))])

            def p1_B(t):
                b = t % 2
                compn(ph, "scalar", lambda E, b=b: E.activation(rstd[b][:], mv[b][:, 1:2], AF.Ln, bias=epsT[:]),
                      waits=[(DVE.h, t_aggr[t])])
                t_rstd[t] = comp(ph, "scalar", ACT, lambda E, b=b: E.activation(rstd[b][:], rstd[b][:], AF.Exp, scale=-0.5))
                return t_rstd[t]

            def p1_C(t, tb):
                b = t % 2
                bx = t % 4
                t_xn[t] = comp(ph, "vector", DVE,
                               lambda E, b=b, bx=bx: E.tensor_scalar(xnf[b][:], xt[bx][:], mv[b][:, 0:1], rstd[b][:], ALU.subtract, ALU.mult),
                               waits=[(ACT.h, tb), (PE.h, t_trlast.get(t - 2, 0))])

            def p1_D(t):
                b = t % 2
                ti = t % 8
                for g in range(NG):
                    bank = S2[g % 2]
                    wt = [(DVE.h, t_xn[t]), (s2war_eng[g % 2], s2war[g % 2])]
                    def trs(E, g=g, bank=bank, b=b):
                        r = None
                        for k in range(PER):
                            dc = g * PER + k
                            r = E.transpose(bank[:, 128 * k:128 * (k + 1)],
                                            xnf[b][:, 128 * dc:128 * (dc + 1)], identF[:])
                        return r
                    tpe = comp(ph, "tensor", PE, trs, waits=wt)
                    t_trlast[t] = tpe
                    x3 = xnTs[:].rearrange("p (dc tok) -> p dc tok", tok=1024)
                    dst = x3[:, g * PER:(g + 1) * PER, ti * 128:(ti + 1) * 128]
                    srcv = bank[:, 0:128 * PER].rearrange("p (dc k) -> p dc k", k=128)
                    t_ev[(t, g)] = comp(ph, "scalar", ACT,
                                        lambda E, dst=dst, srcv=srcv: E.copy(dst, srcv),
                                        waits=[(PE.h, tpe)])
                    s2war[g % 2] = t_ev[(t, g)]
                    s2war_eng[g % 2] = ACT.h

            def p1_slab(s):
                evA = t_ev[(s * 8 + 7, NG - 1)]
                QKbanks = [W1b, W2b, Zb, SU]
                for gi, (pi, e) in enumerate([(p, e) for p in (0, 1) for e in range(c.HPC)]):
                    idx = e * 3 + pi
                    bank0 = QKbanks[(2 * gi) % 4]
                    bank1 = QKbanks[(2 * gi + 1) % 4]
                    wt = [(ACT.h, evA), (POOL.h, pool_p0)]
                    for bk in (bank0, bank1):
                        if id(bk) in qkwar:
                            wt.append(qkwar[id(bk)])
                    def qkmm(E, idx=idx, bank0=bank0, bank1=bank1):
                        r = None
                        for dc in range(c.DC):
                            lhs = wqkv[:, idx * c.D + dc * 128: idx * c.D + dc * 128 + 128]
                            E.matmul(bank0[:, 0:512], lhs,
                                     xnTs[:, dc * 1024: dc * 1024 + 512],
                                     start=(dc == 0), stop=(dc == c.DC - 1))
                            r = E.matmul(bank1[:, 0:512], lhs,
                                         xnTs[:, dc * 1024 + 512: dc * 1024 + 1024],
                                         start=(dc == 0), stop=(dc == c.DC - 1))
                        return r
                    tq = comp(ph, "tensor", PE, qkmm, waits=wt)
                    dstT = QTs if pi == 0 else KTs
                    for tc, bank in ((0, bank0), (1, bank1)):
                        col = e * c.S + s * 1024 + tc * 512
                        te_ = comp(ph, "scalar", ACT,
                                   lambda E, dstT=dstT, col=col, bank=bank: E.copy(dstT[:, col:col + 512], bank[:, 0:512]),
                                   waits=[(PE.h, tq)])
                        qkwar[id(bank)] = (ACT.h, te_)
                vjobs = [(tc, e) for e in range(c.HPC) for tc in range(2)]
                vbanks = [(S2a, 0), (S2a, 512), (S2b, 0), (S2b, 512)]
                vcp = {}
                vtrt = {}

                def v_mm2(e):
                    idx = e * 3 + 2
                    wt = [(ACT.h, t_ev[(s * 8 + 7, NG - 1)]), (POOL.h, pool_p0)]
                    for bk in (W1b, W2b):
                        if id(bk) in qkwar:
                            wt.append(qkwar[id(bk)])
                    def vmm(E, idx=idx):
                        r = None
                        for dc in range(c.DC):
                            lhs = wqkv[:, idx * c.D + dc * 128: idx * c.D + dc * 128 + 128]
                            E.matmul(W1b[:, 0:512], lhs,
                                     xnTs[:, dc * 1024: dc * 1024 + 512],
                                     start=(dc == 0), stop=(dc == c.DC - 1))
                            r = E.matmul(W2b[:, 0:512], lhs,
                                         xnTs[:, dc * 1024 + 512: dc * 1024 + 1024],
                                         start=(dc == 0), stop=(dc == c.DC - 1))
                        return r
                    tv = comp(ph, "tensor", PE, vmm, waits=wt)
                    for tc, bank in ((0, W1b), (1, W2b)):
                        k = e * 2 + tc
                        tvc = comp(ph, "vector", DVE,
                                   lambda E, k=k, bank=bank: E.tensor_copy(vstg[k % 2][:], bank[:, 0:512]),
                                   waits=[(PE.h, tv), (PE.h, vtrt.get(k - 2, 0))])
                        qkwar[id(bank)] = (DVE.h, tvc)
                        vcp[k] = tvc

                def v_tr(k):
                    e, tc = vjobs[k][1], vjobs[k][0]
                    bank, off = vbanks[k]
                    gkey = k % 2 if bank is S2a else k % 2
                    wt2 = [(DVE.h, vcp[k]), (s2war_eng[0 if bank is S2a else 1], s2war[0 if bank is S2a else 1])]
                    def vtr(E, bank=bank, off=off, k=k):
                        r = None
                        for q in range(4):
                            r = E.transpose(bank[:, off + 128 * q:off + 128 * (q + 1)],
                                            vstg[k % 2][:, 128 * q:128 * (q + 1)], identF[:])
                        return r
                    tvt = comp(ph, "tensor", PE, vtr, waits=wt2)
                    vtrt[k] = tvt
                    t_trlast[s * 8 + 7] = tvt
                    col0 = (e * c.NKT + s * 8 + tc * 4) * 128
                    tve = comp(ph, "scalar", ACT,
                               lambda E, col0=col0, bank=bank, off=off: E.copy(VTs[:, col0:col0 + 512], bank[:, off:off + 512]),
                               waits=[(PE.h, tvt)])
                    s2war[0 if bank is S2a else 1] = tve
                    s2war_eng[0 if bank is S2a else 1] = ACT.h

                v_mm2(0)
                v_tr(0)
                v_tr(1)
                if c.HPC == 2:
                    v_mm2(1)
                    v_tr(2)
                    v_tr(3)

            rstdt = {}
            for t in range(c.NT + 2):
                if 1 <= t <= c.NT:
                    rstdt[t - 1] = p1_B(t - 1)
                if 2 <= t <= c.NT + 1:
                    tt = t - 2
                    p1_C(tt, rstdt.pop(tt))
                    p1_D(tt)
                    if tt % 8 == 7:
                        p1_slab(tt // 8)
                if t < c.NT:
                    p1_A(t)

            flush(block, ph)
            s_p1.close()

            s_p2 = contextlib.ExitStack()
            probs = [s_p2.enter_context(nc.sbuf_tensor(f"probs{i}", [128, 1024], BF16)) for i in range(2)]
            Zall = s_p2.enter_context(nc.sbuf_tensor("Zall", [128, c.HPC * c.S], BF16))
            Ztmp = s_p2.enter_context(nc.sbuf_tensor("Ztmp", [128, 512], F32))
            sumr = s_p2.enter_context(nc.sbuf_tensor("sumr", [1, 512], F32))
            recr = s_p2.enter_context(nc.sbuf_tensor("recr", [1, 512], F32))
            sstg = [s_p2.enter_context(nc.sbuf_tensor(f"sstg{i}", [128, c.D], BF16)) for i in range(2)]
            wot = s_p2.enter_context(nc.sbuf_tensor("wot", [128, c.HPC * c.D], BF16))
            wstg = s_p2.enter_context(nc.sbuf_tensor("wstg", [128, c.D], F32))
            masks = [s_p2.enter_context(nc.sbuf_tensor(f"mask{j}", [128, 512], BF16)) for j in range(4)]

            ph = Phase()
            barrier(ph)
            for j in range(4):
                compn(ph, "gpsimd", lambda E, j=j: E.memset(masks[j][:], 1.0))
                compn(ph, "gpsimd", lambda E, j=j: E.affine_select(
                    out=masks[j][:], in_=masks[j][:], compare_op=ALU.is_ge, fill=0.0,
                    base=-128 * j, pattern=[[1, 512]], channel_multiplier=-1))
            t_masks = POOL.n
            tcw = 0
            for e in range(c.HPC):
                dma(ph, "sync", DW, wstg[:], wotT[e], waits=[(POOL.h, tcw)] if e else ())
                tcw = comp(ph, "gpsimd", POOL,
                           lambda E, e=e: E.tensor_copy(wot[:, e * c.D:(e + 1) * c.D], wstg[:]),
                           waits=[(DW.h, DW.n)])
            pool_wot = tcw

            s2xwar = {0: 0, 1: 0}
            probswar = {0: 0, 1: 0}
            t_zfree = 0
            t_sufree = 0
            carry = [None]

            def flush_evac():
                if carry[0] is None:
                    return
                trc, col = carry[0]
                carry[0] = None
                trb = comp(ph, "tensor", PE,
                           lambda E: E.matmul(SU[:, 0:512], ones1f[:], recr[:], start=True, stop=True),
                           waits=[(DVE.h, trc)])
                tzn = comp(ph, "vector", DVE,
                           lambda E, col=col: E.tensor_mul(Zall[:, col:col + 512], Ztmp[:], SU[:, 0:512]),
                           waits=[(PE.h, trb)])
                return tzn

            for qc in range(c.QC):
                nk = 4 * qc + 4
                npair = nk // 2
                for e in range(c.HPC):
                    myexp = {}
                    t_mask = {}
                    t_avs = {}
                    for p in range(npair + 1):
                        if p < npair:
                            Sx = S2[p % 2]
                            wt = [(ACT.h, s2xwar[p % 2])]
                            if qc == 0 and p == 0:
                                wt.append((POOL.h, t_masks))
                            def smm(E, p=p, Sx=Sx, e=e, qc=qc):
                                r = None
                                for half in range(2):
                                    kt = 2 * p + half
                                    r = E.matmul(Sx[:, half * 512:(half + 1) * 512],
                                                 KTs[:, e * c.S + kt * 128: e * c.S + kt * 128 + 128],
                                                 QTs[:, e * c.S + qc * 512: e * c.S + qc * 512 + 512],
                                                 start=True, stop=True)
                                return r
                            ts_ = comp(ph, "tensor", PE, smm, waits=wt)
                            if p == 0:
                                tzn = flush_evac()
                                if tzn is not None:
                                    t_sufree = tzn
                            pb = p % 2
                            wt = [(PE.h, ts_), (PE.h, probswar[pb])]
                            myexp[p] = comp(ph, "scalar", ACT,
                                            lambda E, pb=pb, Sx=Sx: E.activation(probs[pb][:], Sx[:], AF.Exp, scale=ISQ),
                                            waits=wt)
                            s2xwar[p % 2] = myexp[p]
                            dmt = 0
                            for half in range(2):
                                kt = 2 * p + half
                                if kt >= nk - 4:
                                    j = kt - (nk - 4)
                                    dmt = comp(ph, "vector", DVE,
                                               lambda E, pb=pb, half=half, j=j: E.tensor_mul(
                                                   probs[pb][:, half * 512:(half + 1) * 512],
                                                   probs[pb][:, half * 512:(half + 1) * 512],
                                                   masks[j][:]),
                                               waits=[(ACT.h, myexp[p])])
                            t_mask[p] = dmt
                        if p > 0:
                            pp = p - 1
                            pb = pp % 2
                            wt = [(ACT.h, myexp[pp])]
                            if t_mask.get(pp):
                                wt.append((DVE.h, t_mask[pp]))
                            if pp == 0:
                                wt.append((DVE.h, t_zfree))
                                wt.append((DVE.h, t_sufree))
                            def avmm(E, pp=pp, pb=pb, e=e, nk=nk):
                                r = None
                                for half in range(2):
                                    kt = 2 * pp + half
                                    E.matmul(Zb[:, 0:512],
                                             VTs[:, (e * c.NKT + kt) * 128:(e * c.NKT + kt) * 128 + 128],
                                             probs[pb][:, half * 512:(half + 1) * 512],
                                             start=(kt == 0), stop=(kt == nk - 1))
                                    r = E.matmul(SU[0:1, 0:512], ones128b[:],
                                                 probs[pb][:, half * 512:(half + 1) * 512],
                                                 start=(kt == 0), stop=(kt == nk - 1))
                                return r
                            t_avs[pp] = comp(ph, "tensor", PE, avmm, waits=wt)
                            probswar[pb] = t_avs[pp]
                    tlast = t_avs[npair - 1]
                    t_zfree = comp(ph, "vector", DVE, lambda E: E.tensor_copy(Ztmp[:], Zb[:, 0:512]),
                                   waits=[(PE.h, tlast)])
                    compn(ph, "vector", lambda E: E.tensor_copy(sumr[:], SU[0:1, 0:512]))
                    trc = comp(ph, "vector", DVE, lambda E: E.reciprocal(recr[:], sumr[:]))
                    carry[0] = (trc, e * c.S + qc * 512)
            tzn = flush_evac()
            t_zn_final = tzn
            dma(ph, "sync", DW, z_d[:, :], Zall[:], waits=[(DVE.h, t_zn_final)])
            flush(block, ph)
            s_p2.close()
            s_qkv.close()

            s_acc = contextlib.ExitStack()
            acc = s_acc.enter_context(nc.sbuf_tensor("acc", [128, c.TT * c.D], F32))
            xnTo = s_acc.enter_context(nc.sbuf_tensor("xnTo", [128, c.DC * c.TOK], BF16))
            g18T = s_acc.enter_context(nc.sbuf_tensor("g18T", [c.NE, c.TOK], F32))
            b1s = s_acc.enter_context(nc.sbuf_tensor("b1s", [128, c.NE * c.FFC], F32))
            b2s = s_acc.enter_context(nc.sbuf_tensor("b2s", [c.NE, c.D], F32))
            selc = s_acc.enter_context(nc.sbuf_tensor("selc", [c.NE, c.NE], F32))

            s_p25 = contextlib.ExitStack()
            xt2 = [s_p25.enter_context(nc.sbuf_tensor(f"xt2{i}", [128, c.D], F32)) for i in range(4)]
            xnf2l = [s_p25.enter_context(nc.sbuf_tensor(f"xnf2{i}", [128, c.D], F32)) for i in range(2)]
            loT = s_p25.enter_context(nc.sbuf_tensor("loT", [128, c.D], BF16))
            cent = s_p25.enter_context(nc.sbuf_tensor("cent", [128, c.DC * c.NR], F32))
            centh = s_p25.enter_context(nc.sbuf_tensor("centh", [128, c.DC * c.NR], BF16))
            centl = s_p25.enter_context(nc.sbuf_tensor("centl", [128, c.DC * c.NR], BF16))
            scs = s_p25.enter_context(nc.sbuf_tensor("scs", [128, c.TT * c.NR], F32))
            rtmp = s_p25.enter_context(nc.sbuf_tensor("rtmp", [c.NR, 128], F32))
            gwork = s_p25.enter_context(nc.sbuf_tensor("gwork", [128, c.NR], F32))
            gm = s_p25.enter_context(nc.sbuf_tensor("gm", [128, 6], F32))
            geq = s_p25.enter_context(nc.sbuf_tensor("geq", [128, c.NR], F32))
            g18 = s_p25.enter_context(nc.sbuf_tensor("g18", [128, c.NE], F32))
            wot2 = s_p25.enter_context(nc.sbuf_tensor("wot2", [128, c.HPC * c.D], BF16))
            wst2 = s_p25.enter_context(nc.sbuf_tensor("wst2", [128, c.D], F32))
            zbuf = [s_p25.enter_context(nc.sbuf_tensor(f"zbuf{i}", [128, c.HPC * 512], BF16)) for i in range(2)]
            sst2 = [s_p25.enter_context(nc.sbuf_tensor(f"sst2{i}", [128, c.D], BF16)) for i in range(2)]

            ph = Phase()
            barrier(ph)
            tc2 = 0
            for e in range(c.HPC):
                dma(ph, "sync", DW, wst2[:], wotT[e], waits=[(POOL.h, tc2)] if e else ())
                tc2 = comp(ph, "gpsimd", POOL,
                           lambda E, e=e: E.tensor_copy(wot2[:, e * c.D:(e + 1) * c.D], wst2[:]),
                           waits=[(DW.h, DW.n)])
            pool_wot2 = tc2
            t_sev2 = {}
            t_zld = {}
            wo_widx = [0]

            def wo_chunk(qc):
                zb = qc % 2
                wl = [(PE.h, t_zwar.get(qc - 2, 0))]
                for e in range(c.HPC):
                    dma(ph, "sync", DF[qc % 2], zbuf[zb][:, e * 512:(e + 1) * 512],
                        z_d[:, e * c.S + qc * 512: e * c.S + qc * 512 + 512],
                        waits=wl if e == 0 else ())
                t_zld[qc] = (DF[qc % 2].h, DF[qc % 2].n)
                for qsub in range(4):
                    sb = (qc * 4 + qsub) % 2
                    for db in range(c.DB512):
                        widx = wo_widx[0]
                        Wp = WOp[widx % 2]
                        wt = [(DVE.h, t_sev2.get(widx - 2, 0)), (POOL.h, pool_wot2), t_zld[qc]]
                        def womm(E, qsub=qsub, db=db, Wp=Wp, zb=zb):
                            r = None
                            for e in range(c.HPC):
                                r = E.matmul(Wp[:, 0:512],
                                             zbuf[zb][:, e * 512 + qsub * 128: e * 512 + qsub * 128 + 128],
                                             wot2[:, e * c.D + db * 512: e * c.D + db * 512 + 512],
                                             start=(e == 0), stop=(e == c.HPC - 1))
                            return r
                        t_wo = comp(ph, "tensor", PE, womm, waits=wt)
                        t_zwar[qc] = t_wo
                        wt2 = [(PE.h, t_wo), (DS[sb].h, DS[sb].n)]
                        t_sev2[widx] = comp(ph, "vector", DVE,
                                            lambda E, sb=sb, db=db, Wp=Wp: E.tensor_copy(
                                                sst2[sb][:, db * 512:(db + 1) * 512], Wp[:, 0:512]),
                                            waits=wt2)
                        wo_widx[0] += 1
                    row = qc * 512 + qsub * 128
                    dma(ph, "scalar", DS[sb], sp_d[row:row + 128, :], sst2[sb][:],
                        waits=[(DVE.h, t_sev2[wo_widx[0] - 1])])
            t_zwar = {}
            dma(ph, "sync", DY[0], cent[:], centT[:, :])
            dma(ph, "sync", DY[1], b1s[:], b1R[:, :])
            dma(ph, "sync", DX[0], b2s[:], b2R[:, :])
            tch = comp(ph, "gpsimd", POOL, lambda E: E.tensor_copy(centh[:], cent[:]),
                       waits=[(DY[0].h, DY[0].n), (DY[1].h, DY[1].n), (DX[0].h, DX[0].n)])
            tcl = comp(ph, "vector", DVE, lambda E: E.tensor_sub(centl[:], cent[:], centh[:]),
                       waits=[(POOL.h, tch)])
            compn(ph, "gpsimd", lambda E: E.memset(selc[:], 0.0))
            comp(ph, "gpsimd", POOL, lambda E: E.affine_select(
                out=selc[:], in_=selc[:], compare_op=ALU.not_equal, fill=1.0,
                base=0, pattern=[[-1, c.NE]], channel_multiplier=1))

            t_txn = {}
            t_hiT = {}
            t_loT = {}
            t_trp = {}
            t_acc0 = {}
            t_rev = {}
            t_rtr = {}
            t_rmm = {}
            t_aggr2 = {}
            s2war = {0: 0, 1: 0}

            def p25_A(t):
                b = t % 4
                dma(ph, "sync", DX[b], xt2[b][:], xown[t * 128:(t + 1) * 128, :],
                    waits=[(DVE.h, t_txn.get(t - 4, 0))])
                for i in range(ND512):
                    compn(ph, "vector",
                          lambda E, b=b, i=i: E.bn_stats(bn6[:, 6 * i:6 * (i + 1)], xt2[b][:, 512 * i:512 * (i + 1)]),
                          waits=[(DX[b].h, DX[b].n)])
                t_aggr2[t] = comp(ph, "vector", DVE, lambda E, bm=t % 2: E.bn_aggr(mv[bm][:], bn6[:]))

            def p25_B(t):
                b = t % 2
                compn(ph, "scalar", lambda E, b=b: E.activation(rstd[b][:], mv[b][:, 1:2], AF.Ln, bias=epsT[:]),
                      waits=[(DVE.h, t_aggr2[t])])
                return comp(ph, "scalar", ACT, lambda E, b=b: E.activation(rstd[b][:], rstd[b][:], AF.Exp, scale=-0.5))

            def p25_CD(t, tb):
                b = t % 2
                bx = t % 4
                t_txn[t] = comp(ph, "vector", DVE,
                                lambda E, b=b, bx=bx: E.tensor_scalar(xnf2l[b][:], xt2[bx][:], mv[b][:, 0:1], rstd[b][:], ALU.subtract, ALU.mult),
                                waits=[(ACT.h, tb), (PE.h, t_trp.get((t - 2, NG - 1), 0))])
                for g in range(NG):
                    bank = S2[g % 2]
                    wt = [(DVE.h, t_txn[t]), (DVE.h, s2war[g % 2])]
                    def trs(E, g=g, bank=bank, b=b):
                        r = None
                        for k in range(PER):
                            dc = g * PER + k
                            r = E.transpose(bank[:, 128 * k:128 * (k + 1)],
                                            xnf2l[b][:, 128 * dc:128 * (dc + 1)], identF[:])
                        return r
                    t_trp[(t, g)] = comp(ph, "tensor", PE, trs, waits=wt)
                    x3 = xnTo[:].rearrange("p (dc tok) -> p dc tok", tok=c.TOK)
                    a3 = acc[:].rearrange("p (dc tok) -> p dc tok", tok=c.TOK)
                    dsta = a3[:, g * PER:(g + 1) * PER, t * 128:(t + 1) * 128]
                    dsth = x3[:, g * PER:(g + 1) * PER, t * 128:(t + 1) * 128]
                    src3 = bank[:, 0:128 * PER].rearrange("p (dc k) -> p dc k", k=128)
                    t_acc0[(t, g)] = comp(ph, "scalar", ACT,
                                          lambda E, dsta=dsta, src3=src3: E.copy(dsta, src3),
                                          waits=[(PE.h, t_trp[(t, g)])])
                    t_hiT[(t, g)] = comp(ph, "scalar", ACT,
                                         lambda E, dsth=dsth, src3=src3: E.copy(dsth, src3),
                                         waits=[(PE.h, t_trp[(t, g)])])
                    l3 = loT[:].rearrange("p (dc k) -> p dc k", k=128)
                    dstl = l3[:, g * PER:(g + 1) * PER, :]
                    t_loT[(t, g)] = comp(ph, "vector", DVE,
                                         lambda E, dstl=dstl, src3=src3, dsth=dsth: E.tensor_sub(dstl, src3, dsth),
                                         waits=[(ACT.h, t_hiT[(t, g)]),
                                                (PE.h, t_rmm.get(t - 1, 0))])
                    s2war[g % 2] = t_loT[(t, g)]
                wt = [(POOL.h, tch), (DVE.h, tcl),
                      (ACT.h, t_hiT[(t, NG - 1)]), (DVE.h, t_loT[(t, NG - 1)]),
                      (DVE.h, t_rev.get(t - 1, 0))]
                def rmm(E, t=t):
                    r = None
                    n = 0
                    for (use_lo, cc_) in ((0, centh), (1, centh), (0, centl)):
                        for dc in range(c.DC):
                            if use_lo:
                                rhs = loT[:, dc * 128:(dc + 1) * 128]
                            else:
                                rhs = xnTo[:, dc * c.TOK + t * 128: dc * c.TOK + t * 128 + 128]
                            r = E.matmul(Zb[0:c.NR, 0:128], cc_[:, dc * c.NR:(dc + 1) * c.NR], rhs,
                                         start=(n == 0), stop=(n == 3 * c.DC - 1))
                            n += 1
                    return r
                trm = comp(ph, "tensor", PE, rmm, waits=wt)
                t_rmm[t] = trm
                t_rev[t] = comp(ph, "vector", DVE, lambda E: E.tensor_copy(rtmp[:], Zb[0:c.NR, 0:128]),
                                waits=[(PE.h, trm)])
                ttr = comp(ph, "tensor", PE,
                           lambda E: E.transpose(SU[:, 0:c.NR], rtmp[:], identF[0:c.NR, 0:c.NR]),
                           waits=[(DVE.h, t_rev[t]), (DVE.h, t_rtr.get(t - 1, 0))])
                t_rtr[t] = comp(ph, "vector", DVE,
                                lambda E, t=t: E.tensor_copy(scs[:, t * c.NR:(t + 1) * c.NR], SU[:, 0:c.NR]),
                                waits=[(PE.h, ttr)])

            rstdt = {}
            for t in range(c.TT + 2):
                wo_chunk(2 * t) if 2 * t < c.QC else None
                if 1 <= t <= c.TT:
                    rstdt[t - 1] = p25_B(t - 1)
                wo_chunk(2 * t + 1) if 2 * t + 1 < c.QC else None
                if 2 <= t <= c.TT + 1:
                    p25_CD(t - 2, rstdt.pop(t - 2))
                if t < c.TT:
                    p25_A(t)
            for qcq in range(2 * (c.TT + 2), c.QC):
                wo_chunk(qcq)
            ph.do("gpsimd", lambda E, v=DS[0].n: E.wait_ge(DS[0].h, v))
            ph.do("gpsimd", lambda E, v=DS[1].n: E.wait_ge(DS[1].h, v))
            comp(ph, "gpsimd", CC, lambda E: E.collective_compute(
                "ReduceScatter", ALU.add, replica_groups=[list(range(c.R))],
                ins=[sp_d.ap().opt()], outs=[so_d.ap().opt()]))
            tsg = comp(ph, "scalar", ACT, lambda E: E.activation(scs[:], scs[:], AF.Sigmoid),
                       waits=[(DVE.h, t_rtr[c.TT - 1])])
            t_gcp = {}
            t_gtr = {}
            for t in range(c.TT):
                sl = scs[:, t * c.NR:(t + 1) * c.NR]
                compn(ph, "vector", lambda E, sl=sl: E.tensor_add(sl, sl, biasbc[:]),
                      waits=[(ACT.h, tsg), (PE.h, t_gtr.get(t - 1, 0))])
                compn(ph, "vector", lambda E, sl=sl: E.tensor_copy(gwork[:], sl))
                for r_ in range(4):
                    compn(ph, "vector", lambda E, r_=r_: E.reduce_max(gm[:, r_:r_ + 1], gwork[:], axis=AX.X))
                    if r_ < 3:
                        compn(ph, "vector", lambda E, r_=r_: E.tensor_scalar(geq[:], gwork[:], gm[:, r_:r_ + 1], None, ALU.is_equal))
                        compn(ph, "vector", lambda E: E.scalar_tensor_tensor(gwork[:], geq[:], -1e30, gwork[:], ALU.mult, ALU.add))
                compn(ph, "vector", lambda E: E.tensor_add(gm[:, 4:5], gm[:, 0:1], gm[:, 1:2]))
                compn(ph, "vector", lambda E: E.tensor_add(gm[:, 4:5], gm[:, 4:5], gm[:, 2:3]))
                compn(ph, "vector", lambda E: E.tensor_add(gm[:, 4:5], gm[:, 4:5], gm[:, 3:4]))
                compn(ph, "vector", lambda E: E.tensor_scalar_max(gm[:, 4:5], gm[:, 4:5], 1e-8))
                compn(ph, "vector", lambda E: E.reciprocal(gm[:, 5:6], gm[:, 4:5]))
                compn(ph, "vector", lambda E, sl=sl: E.tensor_scalar(geq[:], sl, gm[:, 3:4], None, ALU.is_ge))
                compn(ph, "vector", lambda E: E.memset(g18[:, 0:c.NS], 1.0))
                tg = comp(ph, "vector", DVE,
                          lambda E, sl=sl: E.scalar_tensor_tensor(g18[:, c.NS:c.NE], sl, gm[:, 5:6], geq[:], ALU.mult, ALU.mult))
                ttr = comp(ph, "tensor", PE,
                           lambda E: E.transpose(SU[0:c.NE, 0:128], g18[:], identF[:]),
                           waits=[(DVE.h, tg), (DVE.h, t_gcp.get(t - 1, t_rtr[c.TT - 1]))])
                t_gtr[t] = ttr
                t_gcp[t] = comp(ph, "vector", DVE,
                                lambda E, t=t: E.tensor_copy(g18T[:, t * 128:(t + 1) * 128], SU[0:c.NE, 0:128]),
                                waits=[(PE.h, ttr)])
            flush(block, ph)
            s_p25.close()

            s_p3 = contextlib.ExitStack()
            hsb = s_p3.enter_context(nc.sbuf_tensor("hsb", [128, c.FFC * c.TOK], BF16))
            w1l = [s_p3.enter_context(nc.sbuf_tensor(f"w1l{i}", [128, c.D], F32)) for i in range(2)]
            w1bb = [s_p3.enter_context(nc.sbuf_tensor(f"w1bb{i}", [128, c.D], BF16)) for i in range(2)]
            w2l = s_p3.enter_context(nc.sbuf_tensor("w2l", [128, c.FFC * 512], F32))
            w2bb = [s_p3.enter_context(nc.sbuf_tensor(f"w2bb{i}", [128, c.FFC * 512], BF16)) for i in range(2)]
            gsb = s_p3.enter_context(nc.sbuf_tensor("gsb", [128, c.TOK], F32))

            ph = Phase()
            barrier(ph)
            pswar = {}
            t_b2 = {}
            bi = 0
            b2banks = [Zb, SU, W1b, W2b]
            for dcg in range(c.DC):
                for tc in range(c.TCO):
                    bank = b2banks[bi % 4]
                    wt = [pswar[id(bank)]] if id(bank) in pswar else []
                    tmm = comp(ph, "tensor", PE,
                               lambda E, dcg=dcg, tc=tc, bank=bank: E.matmul(
                                   bank[:, 0:512], b2s[:, dcg * 128:(dcg + 1) * 128],
                                   g18T[:, tc * 512:(tc + 1) * 512], start=True, stop=True),
                               waits=wt)
                    tadd = comp(ph, "vector", DVE,
                                lambda E, dcg=dcg, tc=tc, bank=bank: E.tensor_add(
                                    acc[:, dcg * c.TOK + tc * 512: dcg * c.TOK + tc * 512 + 512],
                                    acc[:, dcg * c.TOK + tc * 512: dcg * c.TOK + tc * 512 + 512],
                                    bank[:, 0:512]),
                                waits=[(PE.h, tmm)])
                    pswar[id(bank)] = (DVE.h, tadd)
                    bi += 1
            hbanks = [(S2a, 0), (S2a, 512), (S2b, 0), (S2b, 512)]
            hwar = {0: 0, 1: 0, 2: 0, 3: 0}
            w1war = {0: 0, 1: 0}
            w1lwar = {0: 0, 1: 0}
            w2war = {0: 0, 1: 0}
            w2lwar = [0]

            hb_i = 0
            yi = 0
            t_w2mm_last = 0
            for j in range(c.NE):
                routed = j >= c.NS
                t_w1c = {}
                t_w2c = {}

                def w1_stream(ffc):
                    b = ffc % 2
                    dma(ph, "sync", DF[b], w1l[b][:], w1R[j, ffc],
                        waits=[(ACT.h, w1lwar[b])])
                    t_w1c[ffc] = comp(ph, "scalar", ACT,
                                      lambda E, b=b: E.copy(w1bb[b][:], w1l[b][:]),
                                      waits=[(DF[b].h, DF[b].n), (PE.h, w1war[b])])
                    w1lwar[b] = t_w1c[ffc]

                def w2_stream(pc):
                    b = pc % 2
                    dma(ph, "gpsimd", DMG, w2l[:], w2R[j, pc],
                        waits=[(DVE.h, w2lwar[0])])
                    t_w2c[pc] = comp(ph, "vector", DVE,
                                     lambda E, b=b: E.tensor_copy(w2bb[b][:], w2l[:]),
                                     waits=[(DMG.h, DMG.n), (PE.h, w2war[b])])
                    w2lwar[0] = t_w2c[pc]

                w1_stream(0)
                if c.FFC > 1:
                    w1_stream(1)
                w2_stream(0)
                NPC = c.DB512
                tgev = 0
                if routed:
                    tgl = 0
                    for tc in range(c.TCO):
                        wtz = [pswar[id(Zb)]] if id(Zb) in pswar else []
                        tmm = comp(ph, "tensor", PE,
                                   lambda E, j=j, tc=tc: E.matmul(Zb[0:1, 0:512], selc[:, j:j + 1],
                                                                  g18T[:, tc * 512:(tc + 1) * 512], start=True, stop=True),
                                   waits=wtz)
                        tgl = comp(ph, "vector", DVE,
                                   lambda E, tc=tc: E.tensor_copy(gsb[0:1, tc * 512:tc * 512 + 512], Zb[0:1, 0:512]),
                                   waits=[(PE.h, tmm)])
                        pswar[id(Zb)] = (DVE.h, tgl)
                    for tc in range(c.TCO):
                        wts = [pswar[id(SU)]] if id(SU) in pswar else []
                        tmm = comp(ph, "tensor", PE,
                                   lambda E, tc=tc: E.matmul(SU[:, 0:512], ones1f[:], gsb[0:1, tc * 512:tc * 512 + 512], start=True, stop=True),
                                   waits=[(DVE.h, tgl)] + wts)
                        tgev = comp(ph, "vector", DVE,
                                    lambda E, tc=tc: E.tensor_copy(gsb[:, tc * 512:(tc + 1) * 512], SU[:, 0:512]),
                                    waits=[(PE.h, tmm)])
                        pswar[id(SU)] = (DVE.h, tgev)
                t_hlast = 0
                for ffc in range(c.FFC):
                    slots = []
                    for _tc in range(c.TCO):
                        slots.append(hbanks[hb_i % 4] + (hb_i % 4,))
                        hb_i += 1
                    b = ffc % 2
                    wt = [(ACT.h, t_w1c[ffc])] + [(ACT.h, hwar[sl[2]]) for sl in slots]
                    def hmm(E, slots=tuple(slots), b=b):
                        r = None
                        for dc in range(c.DC):
                            lhs = w1bb[b][:, dc * 128:(dc + 1) * 128]
                            for tc, (bank, off, _s) in enumerate(slots):
                                r = E.matmul(bank[:, off:off + 512], lhs,
                                             xnTo[:, dc * c.TOK + tc * 512: dc * c.TOK + tc * 512 + 512],
                                             start=(dc == 0), stop=(dc == c.DC - 1))
                        return r
                    tmm = comp(ph, "tensor", PE, hmm, waits=wt)
                    w1war[b] = tmm
                    for tc, (bank, off, slot) in enumerate(slots):
                        wt2 = [(PE.h, tmm), (PE.h, t_w2mm_last)]
                        trl = comp(ph, "scalar", ACT,
                                   lambda E, j=j, ffc=ffc, tc=tc, bank=bank, off=off: E.activation(
                                       hsb[:, ffc * c.TOK + tc * 512: ffc * c.TOK + tc * 512 + 512],
                                       bank[:, off:off + 512], AF.Relu,
                                       bias=b1s[:, j * c.FFC + ffc: j * c.FFC + ffc + 1]),
                                   waits=wt2)
                        hwar[slot] = trl
                        if routed:
                            t_hlast = comp(ph, "vector", DVE,
                                           lambda E, ffc=ffc, tc=tc: E.tensor_mul(
                                               hsb[:, ffc * c.TOK + tc * 512: ffc * c.TOK + tc * 512 + 512],
                                               hsb[:, ffc * c.TOK + tc * 512: ffc * c.TOK + tc * 512 + 512],
                                               gsb[:, tc * 512:(tc + 1) * 512]),
                                           waits=[(ACT.h, trl), (DVE.h, tgev)])
                        else:
                            t_hlast = trl
                    if ffc + 2 < c.FFC:
                        w1_stream(ffc + 2)
                hlast_eng = DVE.h if routed else ACT.h
                ybanks = [W1b, W2b, Zb, SU]
                for pc in range(c.DB512):
                    b = pc % 2
                    if pc + 1 < c.DB512:
                        w2_stream(pc + 1)
                    for q in range(4):
                        dcg = pc * 4 + q
                        pair = [ybanks[(yi + tc) % 4] for tc in range(c.TCO)]
                        wt = [(DVE.h, t_w2c[pc]), (hlast_eng, t_hlast)]
                        for bk in pair:
                            if id(bk) in pswar:
                                wt.append(pswar[id(bk)])
                        def ymm(E, b=b, q=q, pair=tuple(pair)):
                            r = None
                            for ffc in range(c.FFC):
                                lhs = w2bb[b][:, ffc * 512 + q * 128: ffc * 512 + q * 128 + 128]
                                for tc, bk in enumerate(pair):
                                    r = E.matmul(bk[:, 0:512], lhs,
                                                 hsb[:, ffc * c.TOK + tc * 512: ffc * c.TOK + tc * 512 + 512],
                                                 start=(ffc == 0), stop=(ffc == c.FFC - 1))
                            return r
                        tmm = comp(ph, "tensor", PE, ymm, waits=wt)
                        w2war[b] = tmm
                        t_w2mm_last = tmm
                        for tc, bk in enumerate(pair):
                            col = dcg * c.TOK + tc * 512
                            tadd = comp(ph, "vector", DVE,
                                        lambda E, col=col, bk=bk: E.tensor_add(
                                            acc[:, col:col + 512], acc[:, col:col + 512], bk[:, 0:512]),
                                        waits=[(PE.h, tmm)])
                            pswar[id(bk)] = (DVE.h, tadd)
                        yi += c.TCO
            ph.do("gpsimd", lambda E: E.wait_ge(CC.h, 1))
            RSD.take(1)
            ph.do("gpsimd", lambda E: E.sem_inc(RSD.h, 1))
            t_rs_done = RSD.n
            flush(block, ph)
            s_p3.close()

            s_p4 = contextlib.ExitStack()
            sown = [s_p4.enter_context(nc.sbuf_tensor(f"sown{i}", [128, c.D], BF16)) for i in range(2)]
            xo4 = [s_p4.enter_context(nc.sbuf_tensor(f"xo4{i}", [128, c.D], F32)) for i in range(2)]
            outb = [s_p4.enter_context(nc.sbuf_tensor(f"outb{i}", [128, c.D], F32)) for i in range(2)]

            ph = Phase()
            barrier(ph)
            t_ycp = {}
            t_sil = {}
            trswar = {0: 0, 1: 0}
            a4 = acc[:].rearrange("p (dc tok) -> p dc tok", tok=c.TOK)
            for t in range(c.TT):
                b = t % 2
                wl = [(RSD.h, t_rs_done), (DVE.h, t_ycp.get(t - 2, 0))]
                dma(ph, "sync", DX[b], sown[b][:], so_d[t * 128:(t + 1) * 128, :], waits=wl)
                dma(ph, "sync", DX[b], xo4[b][:], xown[t * 128:(t + 1) * 128, :])
                acol = a4[:, :, t * 128:(t + 1) * 128]
                t_sil[t] = comp(ph, "scalar", ACT,
                                lambda E, acol=acol: E.activation(acol, acol, AF.Silu))
                tlast = 0
                for g in range(NG):
                    bank = S2[g % 2]
                    wt = [(ACT.h, t_sil[t]), (DVE.h, trswar[g % 2])]
                    def trs(E, g=g, bank=bank, t=t):
                        r = None
                        for k in range(PER):
                            dc = g * PER + k
                            r = E.transpose(bank[:, 128 * k:128 * (k + 1)],
                                            acc[:, dc * c.TOK + t * 128: dc * c.TOK + t * 128 + 128],
                                            identF[:])
                        return r
                    tpe = comp(ph, "tensor", PE, trs, waits=wt)
                    wt2 = [(PE.h, tpe), (DX[b].h, DX[b].n), (DY[b].h, DY[b].n)]
                    cl, cw = g * PER * 128, PER * 128
                    compn(ph, "vector",
                          lambda E, b=b, bank=bank, cl=cl, cw=cw: E.tensor_mul(
                              outb[b][:, cl:cl + cw], sown[b][:, cl:cl + cw], bank[:, 0:cw]),
                          waits=wt2)
                    tlast = comp(ph, "vector", DVE,
                                 lambda E, b=b, cl=cl, cw=cw: E.tensor_add(
                                     outb[b][:, cl:cl + cw], outb[b][:, cl:cl + cw], xo4[b][:, cl:cl + cw]))
                    trswar[g % 2] = tlast
                t_ycp[t] = tlast
                dma(ph, "sync", DY[b], yout[t * 128:(t + 1) * 128, :], outb[b][:],
                    waits=[(DVE.h, tlast)])
            ph.do("sync", lambda E, v0=DY[0].n, v1=DY[1].n: (E.wait_ge(DY[0].h, v0), E.wait_ge(DY[1].h, v1)))
            flush(block, ph)
            s_p4.close()
            s_acc.close()

    return nc



def prep_inputs(c: Cfg, ins):
    f32 = np.float32
    x = np.asarray(ins["L0_attn_x"], f32).reshape(c.S, c.D)
    WK = np.asarray(ins["L0_attn_W_K"], f32)
    WQ = np.asarray(ins["L0_attn_W_Q"], f32)
    WV = np.asarray(ins["L0_attn_W_V"], f32)
    WO = np.asarray(ins["L0_attn_W_O"], f32)
    w1_all = np.concatenate([np.asarray(ins["L0_ffn_shared_w1"], f32),
                             np.asarray(ins["L0_ffn_routed_w1"], f32)], 0)
    b1_all = np.concatenate([np.asarray(ins["L0_ffn_shared_b1"], f32),
                             np.asarray(ins["L0_ffn_routed_b1"], f32)], 0)
    w2_all = np.concatenate([np.asarray(ins["L0_ffn_shared_w2"], f32),
                             np.asarray(ins["L0_ffn_routed_w2"], f32)], 0)
    b2_all = np.concatenate([np.asarray(ins["L0_ffn_shared_b2"], f32),
                             np.asarray(ins["L0_ffn_routed_b2"], f32)], 0)
    cen = np.asarray(ins["L0_ffn_centroids"], f32)
    bias = np.asarray(ins["L0_ffn_bias"], f32)

    w1v = w1_all.reshape(c.NE, c.DC, 128, c.FFC, 128)
    w1Rm = np.ascontiguousarray(w1v.transpose(0, 3, 2, 1, 4).reshape(c.NE, c.FFC, 128, c.D))
    w2v = w2_all.reshape(c.NE, c.FFC, 128, c.DB512, 512)
    w2Rm = np.ascontiguousarray(w2v.transpose(0, 3, 2, 1, 4).reshape(c.NE, c.DB512, 128, c.FFC * 512))
    b1Rm = np.ascontiguousarray(b1_all.reshape(c.NE, c.FFC, 128).transpose(2, 0, 1).reshape(128, c.NE * c.FFC))
    centTm = np.ascontiguousarray(cen.reshape(c.NR, c.DC, 128).transpose(2, 1, 0).reshape(128, c.DC * c.NR))
    biasm = np.ascontiguousarray(bias.reshape(1, c.NR))

    in_maps = []
    for r in range(c.R):
        heads = [r * c.HPC + e for e in range(c.HPC)]
        wqkvT = np.empty((3 * c.HPC, 128, c.D), f32)
        for e, h in enumerate(heads):
            for pi, Wm in ((0, WQ), (1, WK), (2, WV)):
                Wh = Wm[h]
                blk = Wh.reshape(c.DH, c.DC, 128).transpose(2, 1, 0)
                wqkvT[e * 3 + pi] = blk.reshape(128, c.D)
        wotTm = np.empty((c.HPC, 128, c.D), f32)
        for e, h in enumerate(heads):
            wotTm[e] = WO[:, h * c.DH:(h + 1) * c.DH].T
        in_maps.append({
            "xall": x,
            "xown": np.ascontiguousarray(x[r * c.TOK:(r + 1) * c.TOK]),
            "wqkvT": np.ascontiguousarray(wqkvT),
            "wotT": np.ascontiguousarray(wotTm),
            "w1R": w1Rm, "w2R": w2Rm, "b1R": b1Rm, "b2R": b2_all,
            "centT": centTm, "biasr": biasm,
        })
    return in_maps


_CFG = Cfg()


def kernel(**inputs):
    c = _CFG
    in_maps = prep_inputs(c, inputs)
    nc = build(c)
    res = run_bass_kernel_spmd(nc, in_maps, core_ids=list(range(c.R)))
    out = np.concatenate([res.results[r]["y"] for r in range(c.R)], axis=0)
    return out.reshape(1, c.S, c.D).astype(np.float32)


# version 15 (final)
# speedup vs baseline: 1.0293x; 1.0293x over previous
import contextlib
import math
from dataclasses import dataclass

import numpy as np
import ml_dtypes

import concourse.bass as bass
import concourse.mybir as mybir
from concourse.bass_utils import run_bass_kernel_spmd

F32 = mybir.dt.float32
BF16 = mybir.dt.bfloat16
AF = mybir.ActivationFunctionType
ALU = mybir.AluOpType
AX = mybir.AxisListType

ENGINES = ("sync", "tensor", "vector", "scalar", "gpsimd")


@dataclass
class Cfg:
    S: int = 8192
    D: int = 2048
    DFF: int = 1408
    NH: int = 16
    HPC: int = 2
    DH: int = 128
    NS: int = 2
    NR: int = 16
    KR: int = 4
    R: int = 8

    @property
    def DC(self):  return self.D // 128
    @property
    def FFC(self): return self.DFF // 128
    @property
    def NE(self):  return self.NS + self.NR
    @property
    def TOK(self): return self.S // self.R
    @property
    def TT(self):  return self.TOK // 128
    @property
    def NT(self):  return self.S // 128
    @property
    def NKT(self): return self.S // 128
    @property
    def QC(self):  return self.S // 512
    @property
    def SLABS(self): return self.NT // 8
    @property
    def ND256(self): return self.D // 256
    @property
    def DB512(self): return self.D // 512
    @property
    def TCO(self): return self.TOK // 512


class Sem:
    def __init__(self, stack, nc, name):
        self.h = stack.enter_context(nc.semaphore(name))
        self.n = 0
    def take(self, k):
        self.n += k
        return self.n


class Phase:
    def __init__(self):
        self.q = {e: [] for e in ENGINES}
    def do(self, e, fn):
        self.q[e].append(fn)


def flush(block, ph):
    for e in ENGINES:
        fns = ph.q[e]
        if fns:
            getattr(block, e)(lambda E, fns=fns: [f(E) for f in fns])


def build(cfg: Cfg):
    c = cfg
    assert c.KR == 4 and c.DH == 128 and c.D % 512 == 0 and c.TOK % 512 == 0
    assert c.NT % 8 == 0 and c.S % 512 == 0
    ISQ = 1.0 / math.sqrt(float(c.DH))
    ND512 = c.D // 512
    PER = min(8, c.DC)
    NG = max(1, c.DC // PER)

    nc = bass.Bass("TRN2", target_bir_lowering=False)

    xall = nc.dram_tensor("xall", [c.S, c.D], F32, kind="ExternalInput").ap()
    xown = nc.dram_tensor("xown", [c.TOK, c.D], F32, kind="ExternalInput").ap()
    wqkvT = nc.dram_tensor("wqkvT", [3 * c.HPC, 128, c.D], F32, kind="ExternalInput").ap()
    wotT = nc.dram_tensor("wotT", [c.HPC, 128, c.D], F32, kind="ExternalInput").ap()
    w1R = nc.dram_tensor("w1R", [c.NE, c.FFC, 128, c.D], F32, kind="ExternalInput").ap()
    w2R = nc.dram_tensor("w2R", [c.NE, c.DB512, 128, c.FFC * 512], F32, kind="ExternalInput").ap()
    b1R = nc.dram_tensor("b1R", [128, c.NE * c.FFC], F32, kind="ExternalInput").ap()
    b2R = nc.dram_tensor("b2R", [c.NE, c.D], F32, kind="ExternalInput").ap()
    centT = nc.dram_tensor("centT", [128, c.DC * c.NR], F32, kind="ExternalInput").ap()
    biasr = nc.dram_tensor("biasr", [1, c.NR], F32, kind="ExternalInput").ap()
    yout = nc.dram_tensor("y", [c.TOK, c.D], F32, kind="ExternalOutput").ap()
    sp_d = nc.dram_tensor("s_partial", [c.S, c.D], BF16)
    so_d = nc.dram_tensor("s_own", [c.TOK, c.D], BF16)

    stack = contextlib.ExitStack()
    with stack:
        DX = [Sem(stack, nc, f"dx{i}") for i in range(4)]
        DW = Sem(stack, nc, "dw")
        DF = [Sem(stack, nc, f"df{i}") for i in range(2)]
        DMG = Sem(stack, nc, "dmg")
        DS = [Sem(stack, nc, f"ds{i}") for i in range(2)]
        DY = [Sem(stack, nc, f"dy{i}") for i in range(2)]
        PE = Sem(stack, nc, "pe")
        DVE = Sem(stack, nc, "dve")
        ACT = Sem(stack, nc, "act")
        POOL = Sem(stack, nc, "pool")
        CC = Sem(stack, nc, "cc")
        RSD = Sem(stack, nc, "rsd")
        BAR = Sem(stack, nc, "bar")
        ALL_DMA = {"sync": [DX[0], DX[1], DX[2], DX[3], DW, DF[0], DF[1], DY[0], DY[1]],
                   "gpsimd": [DMG], "scalar": [DS[0], DS[1]]}

        SERIAL = {"vector": DVE, "scalar": ACT, "gpsimd": POOL}

        def W(E, waits):
            for (h, v) in waits:
                if v > 0:
                    E.wait_ge(h, v)

        def dma(ph, q, sem, dst, src, waits=()):
            waits = list(waits)
            if q in SERIAL:
                s = SERIAL[q]
                waits.append((s.h, s.n))
            t = sem.take(16)
            def f(E, dst=dst, src=src, sem=sem, waits=tuple(waits)):
                W(E, waits)
                E.dma_start(out=dst, in_=src).then_inc(sem.h, 16)
            ph.do(q, f)
            return t

        def comp(ph, eng, sem, fn, waits=()):
            waits = list(waits)
            if eng in SERIAL:
                s = SERIAL[eng]
                waits.append((s.h, s.n))
            t = sem.take(1)
            def f(E, fn=fn, sem=sem, waits=tuple(waits)):
                W(E, waits)
                fn(E).then_inc(sem.h, 1)
            ph.do(eng, f)
            return t

        def compn(ph, eng, fn, waits=()):
            if eng in SERIAL:
                return comp(ph, eng, SERIAL[eng], fn, waits)
            def f(E, fn=fn, waits=tuple(waits)):
                W(E, waits)
                fn(E)
            ph.do(eng, f)
            return 0

        def barrier(ph):
            for q, sems in ALL_DMA.items():
                for s in sems:
                    ph.do(q, lambda E, s=s, v=s.n: E.wait_ge(s.h, v) if v else None)
            for e, s in (("vector", DVE), ("scalar", ACT), ("gpsimd", POOL), ("tensor", PE)):
                ph.do(e, lambda E, s=s, v=s.n: E.wait_ge(s.h, v) if v else None)
            for e in ENGINES:
                BAR.take(1)
                ph.do(e, lambda E: E.sem_inc(BAR.h, 1))
            tgt = BAR.n
            for e in ENGINES:
                ph.do(e, lambda E, tgt=tgt: E.wait_ge(BAR.h, tgt))

        S2a = stack.enter_context(nc.psum_tensor("S2a", [128, 1024], F32))
        S2b = stack.enter_context(nc.psum_tensor("S2b", [128, 1024], F32))
        Zb = stack.enter_context(nc.psum_tensor("Zb", [128, 512], F32))
        SU = stack.enter_context(nc.psum_tensor("SU", [128, 512], F32))
        W1b = stack.enter_context(nc.psum_tensor("W1b", [128, 512], F32))
        W2b = stack.enter_context(nc.psum_tensor("W2b", [128, 512], F32))
        S2 = [S2a, S2b]
        WOp = [W1b, W2b]

        identF = stack.enter_context(nc.sbuf_tensor("identF", [128, 128], F32))
        ones128b = stack.enter_context(nc.sbuf_tensor("ones128b", [128, 1], BF16))
        ones1f = stack.enter_context(nc.sbuf_tensor("ones1f", [1, 128], F32))
        epsT = stack.enter_context(nc.sbuf_tensor("epsT", [128, 1], F32))
        biasbc = stack.enter_context(nc.sbuf_tensor("biasbc", [128, c.NR], F32))
        mv = [stack.enter_context(nc.sbuf_tensor(f"mv{i}", [128, 2], F32)) for i in range(2)]
        rstd = [stack.enter_context(nc.sbuf_tensor(f"rstd{i}", [128, 1], F32)) for i in range(2)]
        bn6 = stack.enter_context(nc.sbuf_tensor("bn6", [128, 6 * ND512], F32))

        with nc.Block() as block:
            s_qkv = contextlib.ExitStack()
            QTs = s_qkv.enter_context(nc.sbuf_tensor("QTs", [128, c.HPC * c.S], BF16))
            KTs = s_qkv.enter_context(nc.sbuf_tensor("KTs", [128, c.HPC * c.S], BF16))
            VTs = s_qkv.enter_context(nc.sbuf_tensor("VTs", [128, c.HPC * c.S], BF16))

            s_p1 = contextlib.ExitStack()
            xnTs = s_p1.enter_context(nc.sbuf_tensor("xnTs", [128, c.DC * 1024], BF16))
            wqkv = s_p1.enter_context(nc.sbuf_tensor("wqkv", [128, 3 * c.HPC * c.D], BF16))
            xt = [s_p1.enter_context(nc.sbuf_tensor(f"xt{i}", [128, c.D], F32)) for i in range(4)]
            xnf = [s_p1.enter_context(nc.sbuf_tensor(f"xnf{i}", [128, c.D], F32)) for i in range(2)]
            vstg = [s_p1.enter_context(nc.sbuf_tensor(f"vstg{i}", [128, 512], F32)) for i in range(2)]

            ph = Phase()
            compn(ph, "gpsimd", lambda E: E.memset(identF[:], 0.0))
            compn(ph, "gpsimd", lambda E: E.affine_select(
                out=identF[:], in_=identF[:], compare_op=ALU.not_equal, fill=1.0,
                base=0, pattern=[[-1, 128]], channel_multiplier=1))
            compn(ph, "gpsimd", lambda E: E.memset(ones128b[:], 1.0))
            compn(ph, "gpsimd", lambda E: E.memset(ones1f[:], 1.0))
            compn(ph, "gpsimd", lambda E: E.memset(epsT[:], 1e-5))
            t_consts = comp(ph, "gpsimd", POOL, lambda E: E.engine_nop())

            t_cast = {}
            for idx in range(3 * c.HPC):
                b = idx % 4
                wl = [(POOL.h, t_cast.get(idx - 4, 0))]
                dma(ph, "sync", DX[b], xt[b][:], wqkvT[idx], waits=wl)
                t_cast[idx] = comp(
                    ph, "gpsimd", POOL,
                    lambda E, idx=idx, b=b: E.tensor_copy(wqkv[:, idx * c.D:(idx + 1) * c.D], xt[b][:]),
                    waits=[(DX[b].h, DX[b].n), (POOL.h, t_consts)])
            pool_p0 = POOL.n
            dma(ph, "sync", DW, xt[0][0:1, 0:c.NR], biasr[:, :],
                waits=[(POOL.h, POOL.n)])
            tpe0 = comp(ph, "tensor", PE,
                        lambda E: E.matmul(W1b[:, 0:c.NR], ones1f[:], xt[0][0:1, 0:c.NR], start=True, stop=True),
                        waits=[(DW.h, DW.n), (POOL.h, pool_p0)])
            comp(ph, "vector", DVE, lambda E: E.tensor_copy(biasbc[:], W1b[:, 0:c.NR]),
                 waits=[(PE.h, tpe0)])

            t_xn = {}
            t_rstd = {}
            t_ev = {}
            t_trlast = {}
            s2war = {0: 0, 1: 0}
            s2war_eng = {0: ACT.h, 1: ACT.h}
            qkwar = {}
            t_aggr = {}

            def p1_A(t):
                b = t % 4
                wl = [(DVE.h, t_xn.get(t - 4, 0))]
                if t < 4:
                    wl += [(POOL.h, pool_p0), (PE.h, tpe0)]
                dma(ph, "sync", DX[b], xt[b][:], xall[t * 128:(t + 1) * 128, :], waits=wl)
                for i in range(ND512):
                    compn(ph, "vector",
                          lambda E, b=b, i=i: E.bn_stats(bn6[:, 6 * i:6 * (i + 1)], xt[b][:, 512 * i:512 * (i + 1)]),
                          waits=[(DX[b].h, DX[b].n)])
                t_aggr[t] = comp(ph, "vector", DVE, lambda E, bm=t % 2: E.bn_aggr(mv[bm][:], bn6[:]),
                                 waits=[(ACT.h, t_rstd.get(t - 2, 0))])

            def p1_B(t):
                b = t % 2
                compn(ph, "scalar", lambda E, b=b: E.activation(rstd[b][:], mv[b][:, 1:2], AF.Ln, bias=epsT[:]),
                      waits=[(DVE.h, t_aggr[t])])
                t_rstd[t] = comp(ph, "scalar", ACT, lambda E, b=b: E.activation(rstd[b][:], rstd[b][:], AF.Exp, scale=-0.5))
                return t_rstd[t]

            def p1_C(t, tb):
                b = t % 2
                bx = t % 4
                t_xn[t] = comp(ph, "vector", DVE,
                               lambda E, b=b, bx=bx: E.tensor_scalar(xnf[b][:], xt[bx][:], mv[b][:, 0:1], rstd[b][:], ALU.subtract, ALU.mult),
                               waits=[(ACT.h, tb), (PE.h, t_trlast.get(t - 2, 0))])

            def p1_D(t):
                b = t % 2
                ti = t % 8
                for g in range(NG):
                    bank = S2[g % 2]
                    wt = [(DVE.h, t_xn[t]), (s2war_eng[g % 2], s2war[g % 2])]
                    def trs(E, g=g, bank=bank, b=b):
                        r = None
                        for k in range(PER):
                            dc = g * PER + k
                            r = E.transpose(bank[:, 128 * k:128 * (k + 1)],
                                            xnf[b][:, 128 * dc:128 * (dc + 1)], identF[:])
                        return r
                    tpe = comp(ph, "tensor", PE, trs, waits=wt)
                    t_trlast[t] = tpe
                    x3 = xnTs[:].rearrange("p (dc tok) -> p dc tok", tok=1024)
                    dst = x3[:, g * PER:(g + 1) * PER, ti * 128:(ti + 1) * 128]
                    srcv = bank[:, 0:128 * PER].rearrange("p (dc k) -> p dc k", k=128)
                    t_ev[(t, g)] = comp(ph, "scalar", ACT,
                                        lambda E, dst=dst, srcv=srcv: E.copy(dst, srcv),
                                        waits=[(PE.h, tpe)])
                    s2war[g % 2] = t_ev[(t, g)]
                    s2war_eng[g % 2] = ACT.h

            def p1_slab(s):
                evA = t_ev[(s * 8 + 7, NG - 1)]
                QKbanks = [W1b, W2b, Zb, SU]
                for gi, (pi, e) in enumerate([(p, e) for p in (0, 1) for e in range(c.HPC)]):
                    idx = e * 3 + pi
                    bank0 = QKbanks[(2 * gi) % 4]
                    bank1 = QKbanks[(2 * gi + 1) % 4]
                    wt = [(ACT.h, evA), (POOL.h, pool_p0)]
                    for bk in (bank0, bank1):
                        if id(bk) in qkwar:
                            wt.append(qkwar[id(bk)])
                    def qkmm(E, idx=idx, bank0=bank0, bank1=bank1):
                        r = None
                        for dc in range(c.DC):
                            lhs = wqkv[:, idx * c.D + dc * 128: idx * c.D + dc * 128 + 128]
                            E.matmul(bank0[:, 0:512], lhs,
                                     xnTs[:, dc * 1024: dc * 1024 + 512],
                                     start=(dc == 0), stop=(dc == c.DC - 1))
                            r = E.matmul(bank1[:, 0:512], lhs,
                                         xnTs[:, dc * 1024 + 512: dc * 1024 + 1024],
                                         start=(dc == 0), stop=(dc == c.DC - 1))
                        return r
                    tq = comp(ph, "tensor", PE, qkmm, waits=wt)
                    dstT = QTs if pi == 0 else KTs
                    for tc, bank in ((0, bank0), (1, bank1)):
                        col = e * c.S + s * 1024 + tc * 512
                        te_ = comp(ph, "scalar", ACT,
                                   lambda E, dstT=dstT, col=col, bank=bank: E.copy(dstT[:, col:col + 512], bank[:, 0:512]),
                                   waits=[(PE.h, tq)])
                        qkwar[id(bank)] = (ACT.h, te_)
                vjobs = [(tc, e) for e in range(c.HPC) for tc in range(2)]
                vbanks = [(S2a, 0), (S2a, 512), (S2b, 0), (S2b, 512)]
                vcp = {}
                vtrt = {}

                def v_mm2(e):
                    idx = e * 3 + 2
                    wt = [(ACT.h, t_ev[(s * 8 + 7, NG - 1)]), (POOL.h, pool_p0)]
                    for bk in (W1b, W2b):
                        if id(bk) in qkwar:
                            wt.append(qkwar[id(bk)])
                    def vmm(E, idx=idx):
                        r = None
                        for dc in range(c.DC):
                            lhs = wqkv[:, idx * c.D + dc * 128: idx * c.D + dc * 128 + 128]
                            E.matmul(W1b[:, 0:512], lhs,
                                     xnTs[:, dc * 1024: dc * 1024 + 512],
                                     start=(dc == 0), stop=(dc == c.DC - 1))
                            r = E.matmul(W2b[:, 0:512], lhs,
                                         xnTs[:, dc * 1024 + 512: dc * 1024 + 1024],
                                         start=(dc == 0), stop=(dc == c.DC - 1))
                        return r
                    tv = comp(ph, "tensor", PE, vmm, waits=wt)
                    for tc, bank in ((0, W1b), (1, W2b)):
                        k = e * 2 + tc
                        tvc = comp(ph, "vector", DVE,
                                   lambda E, k=k, bank=bank: E.tensor_copy(vstg[k % 2][:], bank[:, 0:512]),
                                   waits=[(PE.h, tv), (PE.h, vtrt.get(k - 2, 0))])
                        qkwar[id(bank)] = (DVE.h, tvc)
                        vcp[k] = tvc

                def v_tr(k):
                    e, tc = vjobs[k][1], vjobs[k][0]
                    bank, off = vbanks[k]
                    gkey = k % 2 if bank is S2a else k % 2
                    wt2 = [(DVE.h, vcp[k]), (s2war_eng[0 if bank is S2a else 1], s2war[0 if bank is S2a else 1])]
                    def vtr(E, bank=bank, off=off, k=k):
                        r = None
                        for q in range(4):
                            r = E.transpose(bank[:, off + 128 * q:off + 128 * (q + 1)],
                                            vstg[k % 2][:, 128 * q:128 * (q + 1)], identF[:])
                        return r
                    tvt = comp(ph, "tensor", PE, vtr, waits=wt2)
                    vtrt[k] = tvt
                    t_trlast[s * 8 + 7] = tvt
                    col0 = (e * c.NKT + s * 8 + tc * 4) * 128
                    tve = comp(ph, "scalar", ACT,
                               lambda E, col0=col0, bank=bank, off=off: E.copy(VTs[:, col0:col0 + 512], bank[:, off:off + 512]),
                               waits=[(PE.h, tvt)])
                    s2war[0 if bank is S2a else 1] = tve
                    s2war_eng[0 if bank is S2a else 1] = ACT.h

                v_mm2(0)
                v_tr(0)
                v_tr(1)
                if c.HPC == 2:
                    v_mm2(1)
                    v_tr(2)
                    v_tr(3)

            rstdt = {}
            for t in range(c.NT + 2):
                if 1 <= t <= c.NT:
                    rstdt[t - 1] = p1_B(t - 1)
                if 2 <= t <= c.NT + 1:
                    tt = t - 2
                    p1_C(tt, rstdt.pop(tt))
                    p1_D(tt)
                    if tt % 8 == 7:
                        p1_slab(tt // 8)
                if t < c.NT:
                    p1_A(t)

            flush(block, ph)
            s_p1.close()

            s_p2 = contextlib.ExitStack()
            probs = [s_p2.enter_context(nc.sbuf_tensor(f"probs{i}", [128, 1024], BF16)) for i in range(2)]
            Zall = s_p2.enter_context(nc.sbuf_tensor("Zall", [128, c.HPC * c.S], BF16))
            Ztmp = s_p2.enter_context(nc.sbuf_tensor("Ztmp", [128, 512], F32))
            sumr = s_p2.enter_context(nc.sbuf_tensor("sumr", [1, 512], F32))
            recr = s_p2.enter_context(nc.sbuf_tensor("recr", [1, 512], F32))
            sstg = [s_p2.enter_context(nc.sbuf_tensor(f"sstg{i}", [128, c.D], BF16)) for i in range(2)]
            wot = s_p2.enter_context(nc.sbuf_tensor("wot", [128, c.HPC * c.D], BF16))
            wstg = s_p2.enter_context(nc.sbuf_tensor("wstg", [128, c.D], F32))
            masks = [s_p2.enter_context(nc.sbuf_tensor(f"mask{j}", [128, 512], BF16)) for j in range(4)]

            ph = Phase()
            barrier(ph)
            for j in range(4):
                compn(ph, "gpsimd", lambda E, j=j: E.memset(masks[j][:], 1.0))
                compn(ph, "gpsimd", lambda E, j=j: E.affine_select(
                    out=masks[j][:], in_=masks[j][:], compare_op=ALU.is_ge, fill=0.0,
                    base=-128 * j, pattern=[[1, 512]], channel_multiplier=-1))
            t_masks = POOL.n
            tcw = 0
            for e in range(c.HPC):
                dma(ph, "sync", DW, wstg[:], wotT[e], waits=[(POOL.h, tcw)] if e else ())
                tcw = comp(ph, "gpsimd", POOL,
                           lambda E, e=e: E.tensor_copy(wot[:, e * c.D:(e + 1) * c.D], wstg[:]),
                           waits=[(DW.h, DW.n)])
            pool_wot = tcw

            s2xwar = {0: 0, 1: 0}
            probswar = {0: 0, 1: 0}
            t_zfree = 0
            t_sufree = 0
            carry = [None]

            def flush_evac():
                if carry[0] is None:
                    return
                trc, col = carry[0]
                carry[0] = None
                trb = comp(ph, "tensor", PE,
                           lambda E: E.matmul(SU[:, 0:512], ones1f[:], recr[:], start=True, stop=True),
                           waits=[(DVE.h, trc)])
                tzn = comp(ph, "vector", DVE,
                           lambda E, col=col: E.tensor_mul(Zall[:, col:col + 512], Ztmp[:], SU[:, 0:512]),
                           waits=[(PE.h, trb)])
                return tzn

            for qc in range(c.QC):
                nk = 4 * qc + 4
                npair = nk // 2
                for e in range(c.HPC):
                    myexp = {}
                    t_mask = {}
                    t_avs = {}
                    for p in range(npair + 1):
                        if p < npair:
                            Sx = S2[p % 2]
                            wt = [(ACT.h, s2xwar[p % 2])]
                            if qc == 0 and p == 0:
                                wt.append((POOL.h, t_masks))
                            def smm(E, p=p, Sx=Sx, e=e, qc=qc):
                                r = None
                                for half in range(2):
                                    kt = 2 * p + half
                                    r = E.matmul(Sx[:, half * 512:(half + 1) * 512],
                                                 KTs[:, e * c.S + kt * 128: e * c.S + kt * 128 + 128],
                                                 QTs[:, e * c.S + qc * 512: e * c.S + qc * 512 + 512],
                                                 start=True, stop=True)
                                return r
                            ts_ = comp(ph, "tensor", PE, smm, waits=wt)
                            if p == 0:
                                tzn = flush_evac()
                                if tzn is not None:
                                    t_sufree = tzn
                            pb = p % 2
                            wt = [(PE.h, ts_), (PE.h, probswar[pb])]
                            myexp[p] = comp(ph, "scalar", ACT,
                                            lambda E, pb=pb, Sx=Sx: E.activation(probs[pb][:], Sx[:], AF.Exp, scale=ISQ),
                                            waits=wt)
                            s2xwar[p % 2] = myexp[p]
                            dmt = 0
                            for half in range(2):
                                kt = 2 * p + half
                                if kt >= nk - 4:
                                    j = kt - (nk - 4)
                                    dmt = comp(ph, "vector", DVE,
                                               lambda E, pb=pb, half=half, j=j: E.tensor_mul(
                                                   probs[pb][:, half * 512:(half + 1) * 512],
                                                   probs[pb][:, half * 512:(half + 1) * 512],
                                                   masks[j][:]),
                                               waits=[(ACT.h, myexp[p])])
                            t_mask[p] = dmt
                        if p > 0:
                            pp = p - 1
                            pb = pp % 2
                            wt = [(ACT.h, myexp[pp])]
                            if t_mask.get(pp):
                                wt.append((DVE.h, t_mask[pp]))
                            if pp == 0:
                                wt.append((DVE.h, t_zfree))
                                wt.append((DVE.h, t_sufree))
                            def avmm(E, pp=pp, pb=pb, e=e, nk=nk):
                                r = None
                                for half in range(2):
                                    kt = 2 * pp + half
                                    E.matmul(Zb[:, 0:512],
                                             VTs[:, (e * c.NKT + kt) * 128:(e * c.NKT + kt) * 128 + 128],
                                             probs[pb][:, half * 512:(half + 1) * 512],
                                             start=(kt == 0), stop=(kt == nk - 1))
                                    r = E.matmul(SU[0:1, 0:512], ones128b[:],
                                                 probs[pb][:, half * 512:(half + 1) * 512],
                                                 start=(kt == 0), stop=(kt == nk - 1))
                                return r
                            t_avs[pp] = comp(ph, "tensor", PE, avmm, waits=wt)
                            probswar[pb] = t_avs[pp]
                    tlast = t_avs[npair - 1]
                    t_zfree = comp(ph, "vector", DVE, lambda E: E.tensor_copy(Ztmp[:], Zb[:, 0:512]),
                                   waits=[(PE.h, tlast)])
                    compn(ph, "vector", lambda E: E.tensor_copy(sumr[:], SU[0:1, 0:512]))
                    trc = comp(ph, "vector", DVE, lambda E: E.reciprocal(recr[:], sumr[:]))
                    carry[0] = (trc, e * c.S + qc * 512)
            tzn = flush_evac()
            t_zn_final = tzn

            t_sevac = {}
            t_store = {}
            widx = 0
            for qc in range(c.QC):
                for qsub in range(4):
                    sb = (qc * 4 + qsub) % 2
                    for db in range(c.DB512):
                        Wp = WOp[widx % 2]
                        wt = [(DVE.h, t_sevac.get(widx - 2, 0)), (POOL.h, pool_wot)]
                        if widx == 0:
                            wt.append((DVE.h, t_zn_final))
                        def womm(E, qc=qc, qsub=qsub, db=db, Wp=Wp):
                            r = None
                            for e in range(c.HPC):
                                r = E.matmul(Wp[:, 0:512],
                                             Zall[:, e * c.S + qc * 512 + qsub * 128: e * c.S + qc * 512 + qsub * 128 + 128],
                                             wot[:, e * c.D + db * 512: e * c.D + db * 512 + 512],
                                             start=(e == 0), stop=(e == c.HPC - 1))
                            return r
                        t_wo = comp(ph, "tensor", PE, womm, waits=wt)
                        wt2 = [(PE.h, t_wo), (DS[sb].h, DS[sb].n)]
                        t_sevac[widx] = comp(ph, "vector", DVE,
                                             lambda E, sb=sb, db=db, Wp=Wp: E.tensor_copy(sstg[sb][:, db * 512:(db + 1) * 512], Wp[:, 0:512]),
                                             waits=wt2)
                        widx += 1
                    row = qc * 512 + qsub * 128
                    dma(ph, "scalar", DS[sb], sp_d[row:row + 128, :], sstg[sb][:],
                        waits=[(DVE.h, t_sevac[widx - 1])])
            ph.do("gpsimd", lambda E, v=DS[0].n: E.wait_ge(DS[0].h, v))
            ph.do("gpsimd", lambda E, v=DS[1].n: E.wait_ge(DS[1].h, v))
            comp(ph, "gpsimd", CC, lambda E: E.collective_compute(
                "ReduceScatter", ALU.add, replica_groups=[list(range(c.R))],
                ins=[sp_d.ap().opt()], outs=[so_d.ap().opt()]))

            flush(block, ph)
            s_p2.close()
            s_qkv.close()

            s_acc = contextlib.ExitStack()
            acc = s_acc.enter_context(nc.sbuf_tensor("acc", [128, c.TT * c.D], F32))
            xnTo = s_acc.enter_context(nc.sbuf_tensor("xnTo", [128, c.DC * c.TOK], BF16))
            g18T = s_acc.enter_context(nc.sbuf_tensor("g18T", [c.NE, c.TOK], F32))
            b1s = s_acc.enter_context(nc.sbuf_tensor("b1s", [128, c.NE * c.FFC], F32))
            b2s = s_acc.enter_context(nc.sbuf_tensor("b2s", [c.NE, c.D], F32))
            selc = s_acc.enter_context(nc.sbuf_tensor("selc", [c.NE, c.NE], F32))

            s_p25 = contextlib.ExitStack()
            xt2 = [s_p25.enter_context(nc.sbuf_tensor(f"xt2{i}", [128, c.D], F32)) for i in range(4)]
            xnf2l = [s_p25.enter_context(nc.sbuf_tensor(f"xnf2{i}", [128, c.D], F32)) for i in range(2)]
            loT = s_p25.enter_context(nc.sbuf_tensor("loT", [128, c.D], BF16))
            cent = s_p25.enter_context(nc.sbuf_tensor("cent", [128, c.DC * c.NR], F32))
            centh = s_p25.enter_context(nc.sbuf_tensor("centh", [128, c.DC * c.NR], BF16))
            centl = s_p25.enter_context(nc.sbuf_tensor("centl", [128, c.DC * c.NR], BF16))
            scs = s_p25.enter_context(nc.sbuf_tensor("scs", [128, c.TT * c.NR], F32))
            rtmp = s_p25.enter_context(nc.sbuf_tensor("rtmp", [c.NR, 128], F32))
            gwork = s_p25.enter_context(nc.sbuf_tensor("gwork", [128, c.NR], F32))
            gm = s_p25.enter_context(nc.sbuf_tensor("gm", [128, 6], F32))
            geq = s_p25.enter_context(nc.sbuf_tensor("geq", [128, c.NR], F32))
            g18 = s_p25.enter_context(nc.sbuf_tensor("g18", [128, c.NE], F32))

            ph = Phase()
            barrier(ph)
            dma(ph, "sync", DW, cent[:], centT[:, :])
            dma(ph, "sync", DW, b1s[:], b1R[:, :])
            dma(ph, "sync", DW, b2s[:], b2R[:, :])
            tch = comp(ph, "gpsimd", POOL, lambda E: E.tensor_copy(centh[:], cent[:]),
                       waits=[(DW.h, DW.n)])
            tcl = comp(ph, "vector", DVE, lambda E: E.tensor_sub(centl[:], cent[:], centh[:]),
                       waits=[(POOL.h, tch)])
            compn(ph, "gpsimd", lambda E: E.memset(selc[:], 0.0))
            comp(ph, "gpsimd", POOL, lambda E: E.affine_select(
                out=selc[:], in_=selc[:], compare_op=ALU.not_equal, fill=1.0,
                base=0, pattern=[[-1, c.NE]], channel_multiplier=1))

            t_txn = {}
            t_hiT = {}
            t_loT = {}
            t_trp = {}
            t_acc0 = {}
            t_rev = {}
            t_rtr = {}
            t_rmm = {}
            t_aggr2 = {}
            s2war = {0: 0, 1: 0}

            def p25_A(t):
                b = t % 4
                dma(ph, "sync", DX[b], xt2[b][:], xown[t * 128:(t + 1) * 128, :],
                    waits=[(DVE.h, t_txn.get(t - 4, 0))])
                for i in range(ND512):
                    compn(ph, "vector",
                          lambda E, b=b, i=i: E.bn_stats(bn6[:, 6 * i:6 * (i + 1)], xt2[b][:, 512 * i:512 * (i + 1)]),
                          waits=[(DX[b].h, DX[b].n)])
                t_aggr2[t] = comp(ph, "vector", DVE, lambda E, bm=t % 2: E.bn_aggr(mv[bm][:], bn6[:]))

            def p25_B(t):
                b = t % 2
                compn(ph, "scalar", lambda E, b=b: E.activation(rstd[b][:], mv[b][:, 1:2], AF.Ln, bias=epsT[:]),
                      waits=[(DVE.h, t_aggr2[t])])
                return comp(ph, "scalar", ACT, lambda E, b=b: E.activation(rstd[b][:], rstd[b][:], AF.Exp, scale=-0.5))

            def p25_CD(t, tb):
                b = t % 2
                bx = t % 4
                t_txn[t] = comp(ph, "vector", DVE,
                                lambda E, b=b, bx=bx: E.tensor_scalar(xnf2l[b][:], xt2[bx][:], mv[b][:, 0:1], rstd[b][:], ALU.subtract, ALU.mult),
                                waits=[(ACT.h, tb), (PE.h, t_trp.get((t - 2, NG - 1), 0))])
                for g in range(NG):
                    bank = S2[g % 2]
                    wt = [(DVE.h, t_txn[t]), (DVE.h, s2war[g % 2])]
                    def trs(E, g=g, bank=bank, b=b):
                        r = None
                        for k in range(PER):
                            dc = g * PER + k
                            r = E.transpose(bank[:, 128 * k:128 * (k + 1)],
                                            xnf2l[b][:, 128 * dc:128 * (dc + 1)], identF[:])
                        return r
                    t_trp[(t, g)] = comp(ph, "tensor", PE, trs, waits=wt)
                    x3 = xnTo[:].rearrange("p (dc tok) -> p dc tok", tok=c.TOK)
                    a3 = acc[:].rearrange("p (dc tok) -> p dc tok", tok=c.TOK)
                    dsta = a3[:, g * PER:(g + 1) * PER, t * 128:(t + 1) * 128]
                    dsth = x3[:, g * PER:(g + 1) * PER, t * 128:(t + 1) * 128]
                    src3 = bank[:, 0:128 * PER].rearrange("p (dc k) -> p dc k", k=128)
                    t_acc0[(t, g)] = comp(ph, "scalar", ACT,
                                          lambda E, dsta=dsta, src3=src3: E.copy(dsta, src3),
                                          waits=[(PE.h, t_trp[(t, g)])])
                    t_hiT[(t, g)] = comp(ph, "scalar", ACT,
                                         lambda E, dsth=dsth, src3=src3: E.copy(dsth, src3),
                                         waits=[(PE.h, t_trp[(t, g)])])
                    l3 = loT[:].rearrange("p (dc k) -> p dc k", k=128)
                    dstl = l3[:, g * PER:(g + 1) * PER, :]
                    t_loT[(t, g)] = comp(ph, "vector", DVE,
                                         lambda E, dstl=dstl, src3=src3, dsth=dsth: E.tensor_sub(dstl, src3, dsth),
                                         waits=[(ACT.h, t_hiT[(t, g)]),
                                                (PE.h, t_rmm.get(t - 1, 0))])
                    s2war[g % 2] = t_loT[(t, g)]
                wt = [(POOL.h, tch), (DVE.h, tcl),
                      (ACT.h, t_hiT[(t, NG - 1)]), (DVE.h, t_loT[(t, NG - 1)]),
                      (DVE.h, t_rev.get(t - 1, 0))]
                def rmm(E, t=t):
                    r = None
                    n = 0
                    for (use_lo, cc_) in ((0, centh), (1, centh), (0, centl)):
                        for dc in range(c.DC):
                            if use_lo:
                                rhs = loT[:, dc * 128:(dc + 1) * 128]
                            else:
                                rhs = xnTo[:, dc * c.TOK + t * 128: dc * c.TOK + t * 128 + 128]
                            r = E.matmul(W1b[0:c.NR, 0:128], cc_[:, dc * c.NR:(dc + 1) * c.NR], rhs,
                                         start=(n == 0), stop=(n == 3 * c.DC - 1))
                            n += 1
                    return r
                trm = comp(ph, "tensor", PE, rmm, waits=wt)
                t_rmm[t] = trm
                t_rev[t] = comp(ph, "vector", DVE, lambda E: E.tensor_copy(rtmp[:], W1b[0:c.NR, 0:128]),
                                waits=[(PE.h, trm)])
                ttr = comp(ph, "tensor", PE,
                           lambda E: E.transpose(W2b[:, 0:c.NR], rtmp[:], identF[0:c.NR, 0:c.NR]),
                           waits=[(DVE.h, t_rev[t]), (DVE.h, t_rtr.get(t - 1, 0))])
                t_rtr[t] = comp(ph, "vector", DVE,
                                lambda E, t=t: E.tensor_copy(scs[:, t * c.NR:(t + 1) * c.NR], W2b[:, 0:c.NR]),
                                waits=[(PE.h, ttr)])

            rstdt = {}
            for t in range(c.TT + 2):
                if 1 <= t <= c.TT:
                    rstdt[t - 1] = p25_B(t - 1)
                if 2 <= t <= c.TT + 1:
                    p25_CD(t - 2, rstdt.pop(t - 2))
                if t < c.TT:
                    p25_A(t)
            tsg = comp(ph, "scalar", ACT, lambda E: E.activation(scs[:], scs[:], AF.Sigmoid),
                       waits=[(DVE.h, t_rtr[c.TT - 1])])
            t_gcp = {}
            t_gtr = {}
            for t in range(c.TT):
                sl = scs[:, t * c.NR:(t + 1) * c.NR]
                compn(ph, "vector", lambda E, sl=sl: E.tensor_add(sl, sl, biasbc[:]),
                      waits=[(ACT.h, tsg), (PE.h, t_gtr.get(t - 1, 0))])
                compn(ph, "vector", lambda E, sl=sl: E.tensor_copy(gwork[:], sl))
                for r_ in range(4):
                    compn(ph, "vector", lambda E, r_=r_: E.reduce_max(gm[:, r_:r_ + 1], gwork[:], axis=AX.X))
                    if r_ < 3:
                        compn(ph, "vector", lambda E, r_=r_: E.tensor_scalar(geq[:], gwork[:], gm[:, r_:r_ + 1], None, ALU.is_equal))
                        compn(ph, "vector", lambda E: E.scalar_tensor_tensor(gwork[:], geq[:], -1e30, gwork[:], ALU.mult, ALU.add))
                compn(ph, "vector", lambda E: E.tensor_add(gm[:, 4:5], gm[:, 0:1], gm[:, 1:2]))
                compn(ph, "vector", lambda E: E.tensor_add(gm[:, 4:5], gm[:, 4:5], gm[:, 2:3]))
                compn(ph, "vector", lambda E: E.tensor_add(gm[:, 4:5], gm[:, 4:5], gm[:, 3:4]))
                compn(ph, "vector", lambda E: E.tensor_scalar_max(gm[:, 4:5], gm[:, 4:5], 1e-8))
                compn(ph, "vector", lambda E: E.reciprocal(gm[:, 5:6], gm[:, 4:5]))
                compn(ph, "vector", lambda E, sl=sl: E.tensor_scalar(geq[:], sl, gm[:, 3:4], None, ALU.is_ge))
                compn(ph, "vector", lambda E: E.memset(g18[:, 0:c.NS], 1.0))
                tg = comp(ph, "vector", DVE,
                          lambda E, sl=sl: E.scalar_tensor_tensor(g18[:, c.NS:c.NE], sl, gm[:, 5:6], geq[:], ALU.mult, ALU.mult))
                ttr = comp(ph, "tensor", PE,
                           lambda E: E.transpose(W2b[0:c.NE, 0:128], g18[:], identF[:]),
                           waits=[(DVE.h, tg), (DVE.h, t_gcp.get(t - 1, t_rtr[c.TT - 1]))])
                t_gtr[t] = ttr
                t_gcp[t] = comp(ph, "vector", DVE,
                                lambda E, t=t: E.tensor_copy(g18T[:, t * 128:(t + 1) * 128], W2b[0:c.NE, 0:128]),
                                waits=[(PE.h, ttr)])
            flush(block, ph)
            s_p25.close()

            s_p3 = contextlib.ExitStack()
            hsb = s_p3.enter_context(nc.sbuf_tensor("hsb", [128, c.FFC * c.TOK], BF16))
            w1l = [s_p3.enter_context(nc.sbuf_tensor(f"w1l{i}", [128, c.D], F32)) for i in range(2)]
            w1bb = [s_p3.enter_context(nc.sbuf_tensor(f"w1bb{i}", [128, c.D], BF16)) for i in range(2)]
            w2l = s_p3.enter_context(nc.sbuf_tensor("w2l", [128, c.FFC * 512], F32))
            w2bb = [s_p3.enter_context(nc.sbuf_tensor(f"w2bb{i}", [128, c.FFC * 512], BF16)) for i in range(2)]
            gsb = s_p3.enter_context(nc.sbuf_tensor("gsb", [128, c.TOK], F32))

            ph = Phase()
            barrier(ph)
            pswar = {}
            t_b2 = {}
            bi = 0
            b2banks = [Zb, SU, W1b, W2b]
            for dcg in range(c.DC):
                for tc in range(c.TCO):
                    bank = b2banks[bi % 4]
                    wt = [pswar[id(bank)]] if id(bank) in pswar else []
                    tmm = comp(ph, "tensor", PE,
                               lambda E, dcg=dcg, tc=tc, bank=bank: E.matmul(
                                   bank[:, 0:512], b2s[:, dcg * 128:(dcg + 1) * 128],
                                   g18T[:, tc * 512:(tc + 1) * 512], start=True, stop=True),
                               waits=wt)
                    tadd = comp(ph, "vector", DVE,
                                lambda E, dcg=dcg, tc=tc, bank=bank: E.tensor_add(
                                    acc[:, dcg * c.TOK + tc * 512: dcg * c.TOK + tc * 512 + 512],
                                    acc[:, dcg * c.TOK + tc * 512: dcg * c.TOK + tc * 512 + 512],
                                    bank[:, 0:512]),
                                waits=[(PE.h, tmm)])
                    pswar[id(bank)] = (DVE.h, tadd)
                    bi += 1
            hbanks = [(S2a, 0), (S2a, 512), (S2b, 0), (S2b, 512)]
            hwar = {0: 0, 1: 0, 2: 0, 3: 0}
            w1war = {0: 0, 1: 0}
            w1lwar = {0: 0, 1: 0}
            w2war = {0: 0, 1: 0}
            w2lwar = [0]

            hb_i = 0
            yi = 0
            t_w2mm_last = 0
            for j in range(c.NE):
                routed = j >= c.NS
                t_w1c = {}
                t_w2c = {}

                def w1_stream(ffc):
                    b = ffc % 2
                    dma(ph, "sync", DF[b], w1l[b][:], w1R[j, ffc],
                        waits=[(ACT.h, w1lwar[b])])
                    t_w1c[ffc] = comp(ph, "scalar", ACT,
                                      lambda E, b=b: E.copy(w1bb[b][:], w1l[b][:]),
                                      waits=[(DF[b].h, DF[b].n), (PE.h, w1war[b])])
                    w1lwar[b] = t_w1c[ffc]

                def w2_stream(pc):
                    b = pc % 2
                    dma(ph, "gpsimd", DMG, w2l[:], w2R[j, pc],
                        waits=[(ACT.h, w2lwar[0])])
                    t_w2c[pc] = comp(ph, "scalar", ACT,
                                     lambda E, b=b: E.copy(w2bb[b][:], w2l[:]),
                                     waits=[(DMG.h, DMG.n), (PE.h, w2war[b])])
                    w2lwar[0] = t_w2c[pc]

                w1_stream(0)
                if c.FFC > 1:
                    w1_stream(1)
                w2_stream(0)
                NPC = c.DB512
                tgev = 0
                if routed:
                    tgl = 0
                    for tc in range(c.TCO):
                        wtz = [pswar[id(Zb)]] if id(Zb) in pswar else []
                        tmm = comp(ph, "tensor", PE,
                                   lambda E, j=j, tc=tc: E.matmul(Zb[0:1, 0:512], selc[:, j:j + 1],
                                                                  g18T[:, tc * 512:(tc + 1) * 512], start=True, stop=True),
                                   waits=wtz)
                        tgl = comp(ph, "vector", DVE,
                                   lambda E, tc=tc: E.tensor_copy(gsb[0:1, tc * 512:tc * 512 + 512], Zb[0:1, 0:512]),
                                   waits=[(PE.h, tmm)])
                        pswar[id(Zb)] = (DVE.h, tgl)
                    for tc in range(c.TCO):
                        wts = [pswar[id(SU)]] if id(SU) in pswar else []
                        tmm = comp(ph, "tensor", PE,
                                   lambda E, tc=tc: E.matmul(SU[:, 0:512], ones1f[:], gsb[0:1, tc * 512:tc * 512 + 512], start=True, stop=True),
                                   waits=[(DVE.h, tgl)] + wts)
                        tgev = comp(ph, "vector", DVE,
                                    lambda E, tc=tc: E.tensor_copy(gsb[:, tc * 512:(tc + 1) * 512], SU[:, 0:512]),
                                    waits=[(PE.h, tmm)])
                        pswar[id(SU)] = (DVE.h, tgev)
                t_hlast = 0
                for ffc in range(c.FFC):
                    slots = []
                    for _tc in range(c.TCO):
                        slots.append(hbanks[hb_i % 4] + (hb_i % 4,))
                        hb_i += 1
                    b = ffc % 2
                    wt = [(ACT.h, t_w1c[ffc])] + [(ACT.h, hwar[sl[2]]) for sl in slots]
                    def hmm(E, slots=tuple(slots), b=b):
                        r = None
                        for dc in range(c.DC):
                            lhs = w1bb[b][:, dc * 128:(dc + 1) * 128]
                            for tc, (bank, off, _s) in enumerate(slots):
                                r = E.matmul(bank[:, off:off + 512], lhs,
                                             xnTo[:, dc * c.TOK + tc * 512: dc * c.TOK + tc * 512 + 512],
                                             start=(dc == 0), stop=(dc == c.DC - 1))
                        return r
                    tmm = comp(ph, "tensor", PE, hmm, waits=wt)
                    w1war[b] = tmm
                    for tc, (bank, off, slot) in enumerate(slots):
                        wt2 = [(PE.h, tmm), (PE.h, t_w2mm_last)]
                        trl = comp(ph, "scalar", ACT,
                                   lambda E, j=j, ffc=ffc, tc=tc, bank=bank, off=off: E.activation(
                                       hsb[:, ffc * c.TOK + tc * 512: ffc * c.TOK + tc * 512 + 512],
                                       bank[:, off:off + 512], AF.Relu,
                                       bias=b1s[:, j * c.FFC + ffc: j * c.FFC + ffc + 1]),
                                   waits=wt2)
                        hwar[slot] = trl
                        if routed:
                            t_hlast = comp(ph, "vector", DVE,
                                           lambda E, ffc=ffc, tc=tc: E.tensor_mul(
                                               hsb[:, ffc * c.TOK + tc * 512: ffc * c.TOK + tc * 512 + 512],
                                               hsb[:, ffc * c.TOK + tc * 512: ffc * c.TOK + tc * 512 + 512],
                                               gsb[:, tc * 512:(tc + 1) * 512]),
                                           waits=[(ACT.h, trl), (DVE.h, tgev)])
                        else:
                            t_hlast = trl
                    if ffc + 2 < c.FFC:
                        w1_stream(ffc + 2)
                hlast_eng = DVE.h if routed else ACT.h
                ybanks = [W1b, W2b, Zb, SU]
                for pc in range(c.DB512):
                    b = pc % 2
                    if pc + 1 < c.DB512:
                        w2_stream(pc + 1)
                    for q in range(4):
                        dcg = pc * 4 + q
                        pair = [ybanks[(yi + tc) % 4] for tc in range(c.TCO)]
                        wt = [(ACT.h, t_w2c[pc]), (hlast_eng, t_hlast)]
                        for bk in pair:
                            if id(bk) in pswar:
                                wt.append(pswar[id(bk)])
                        def ymm(E, b=b, q=q, pair=tuple(pair)):
                            r = None
                            for ffc in range(c.FFC):
                                lhs = w2bb[b][:, ffc * 512 + q * 128: ffc * 512 + q * 128 + 128]
                                for tc, bk in enumerate(pair):
                                    r = E.matmul(bk[:, 0:512], lhs,
                                                 hsb[:, ffc * c.TOK + tc * 512: ffc * c.TOK + tc * 512 + 512],
                                                 start=(ffc == 0), stop=(ffc == c.FFC - 1))
                            return r
                        tmm = comp(ph, "tensor", PE, ymm, waits=wt)
                        w2war[b] = tmm
                        t_w2mm_last = tmm
                        for tc, bk in enumerate(pair):
                            col = dcg * c.TOK + tc * 512
                            tadd = comp(ph, "vector", DVE,
                                        lambda E, col=col, bk=bk: E.tensor_add(
                                            acc[:, col:col + 512], acc[:, col:col + 512], bk[:, 0:512]),
                                        waits=[(PE.h, tmm)])
                            pswar[id(bk)] = (DVE.h, tadd)
                        yi += c.TCO
            ph.do("gpsimd", lambda E: E.wait_ge(CC.h, 1))
            RSD.take(1)
            ph.do("gpsimd", lambda E: E.sem_inc(RSD.h, 1))
            t_rs_done = RSD.n
            flush(block, ph)
            s_p3.close()

            s_p4 = contextlib.ExitStack()
            sown = [s_p4.enter_context(nc.sbuf_tensor(f"sown{i}", [128, c.D], BF16)) for i in range(2)]
            xo4 = [s_p4.enter_context(nc.sbuf_tensor(f"xo4{i}", [128, c.D], F32)) for i in range(2)]
            outb = [s_p4.enter_context(nc.sbuf_tensor(f"outb{i}", [128, c.D], F32)) for i in range(2)]

            ph = Phase()
            barrier(ph)
            t_ycp = {}
            t_sil = {}
            trswar = {0: 0, 1: 0}
            a4 = acc[:].rearrange("p (dc tok) -> p dc tok", tok=c.TOK)
            for t in range(c.TT):
                b = t % 2
                wl = [(RSD.h, t_rs_done), (DVE.h, t_ycp.get(t - 2, 0))]
                dma(ph, "sync", DX[b], sown[b][:], so_d[t * 128:(t + 1) * 128, :], waits=wl)
                dma(ph, "sync", DX[b], xo4[b][:], xown[t * 128:(t + 1) * 128, :])
                acol = a4[:, :, t * 128:(t + 1) * 128]
                t_sil[t] = comp(ph, "scalar", ACT,
                                lambda E, acol=acol: E.activation(acol, acol, AF.Silu))
                tlast = 0
                for g in range(NG):
                    bank = S2[g % 2]
                    wt = [(ACT.h, t_sil[t]), (DVE.h, trswar[g % 2])]
                    def trs(E, g=g, bank=bank, t=t):
                        r = None
                        for k in range(PER):
                            dc = g * PER + k
                            r = E.transpose(bank[:, 128 * k:128 * (k + 1)],
                                            acc[:, dc * c.TOK + t * 128: dc * c.TOK + t * 128 + 128],
                                            identF[:])
                        return r
                    tpe = comp(ph, "tensor", PE, trs, waits=wt)
                    wt2 = [(PE.h, tpe), (DX[b].h, DX[b].n), (DY[b].h, DY[b].n)]
                    cl, cw = g * PER * 128, PER * 128
                    compn(ph, "vector",
                          lambda E, b=b, bank=bank, cl=cl, cw=cw: E.tensor_mul(
                              outb[b][:, cl:cl + cw], sown[b][:, cl:cl + cw], bank[:, 0:cw]),
                          waits=wt2)
                    tlast = comp(ph, "vector", DVE,
                                 lambda E, b=b, cl=cl, cw=cw: E.tensor_add(
                                     outb[b][:, cl:cl + cw], outb[b][:, cl:cl + cw], xo4[b][:, cl:cl + cw]))
                    trswar[g % 2] = tlast
                t_ycp[t] = tlast
                dma(ph, "sync", DY[b], yout[t * 128:(t + 1) * 128, :], outb[b][:],
                    waits=[(DVE.h, tlast)])
            ph.do("sync", lambda E, v0=DY[0].n, v1=DY[1].n: (E.wait_ge(DY[0].h, v0), E.wait_ge(DY[1].h, v1)))
            flush(block, ph)
            s_p4.close()
            s_acc.close()

    return nc



def prep_inputs(c: Cfg, ins):
    f32 = np.float32
    x = np.asarray(ins["L0_attn_x"], f32).reshape(c.S, c.D)
    WK = np.asarray(ins["L0_attn_W_K"], f32)
    WQ = np.asarray(ins["L0_attn_W_Q"], f32)
    WV = np.asarray(ins["L0_attn_W_V"], f32)
    WO = np.asarray(ins["L0_attn_W_O"], f32)
    w1_all = np.concatenate([np.asarray(ins["L0_ffn_shared_w1"], f32),
                             np.asarray(ins["L0_ffn_routed_w1"], f32)], 0)
    b1_all = np.concatenate([np.asarray(ins["L0_ffn_shared_b1"], f32),
                             np.asarray(ins["L0_ffn_routed_b1"], f32)], 0)
    w2_all = np.concatenate([np.asarray(ins["L0_ffn_shared_w2"], f32),
                             np.asarray(ins["L0_ffn_routed_w2"], f32)], 0)
    b2_all = np.concatenate([np.asarray(ins["L0_ffn_shared_b2"], f32),
                             np.asarray(ins["L0_ffn_routed_b2"], f32)], 0)
    cen = np.asarray(ins["L0_ffn_centroids"], f32)
    bias = np.asarray(ins["L0_ffn_bias"], f32)

    w1v = w1_all.reshape(c.NE, c.DC, 128, c.FFC, 128)
    w1Rm = np.ascontiguousarray(w1v.transpose(0, 3, 2, 1, 4).reshape(c.NE, c.FFC, 128, c.D))
    w2v = w2_all.reshape(c.NE, c.FFC, 128, c.DB512, 512)
    w2Rm = np.ascontiguousarray(w2v.transpose(0, 3, 2, 1, 4).reshape(c.NE, c.DB512, 128, c.FFC * 512))
    b1Rm = np.ascontiguousarray(b1_all.reshape(c.NE, c.FFC, 128).transpose(2, 0, 1).reshape(128, c.NE * c.FFC))
    centTm = np.ascontiguousarray(cen.reshape(c.NR, c.DC, 128).transpose(2, 1, 0).reshape(128, c.DC * c.NR))
    biasm = np.ascontiguousarray(bias.reshape(1, c.NR))

    in_maps = []
    for r in range(c.R):
        heads = [r * c.HPC + e for e in range(c.HPC)]
        wqkvT = np.empty((3 * c.HPC, 128, c.D), f32)
        for e, h in enumerate(heads):
            for pi, Wm in ((0, WQ), (1, WK), (2, WV)):
                Wh = Wm[h]
                blk = Wh.reshape(c.DH, c.DC, 128).transpose(2, 1, 0)
                wqkvT[e * 3 + pi] = blk.reshape(128, c.D)
        wotTm = np.empty((c.HPC, 128, c.D), f32)
        for e, h in enumerate(heads):
            wotTm[e] = WO[:, h * c.DH:(h + 1) * c.DH].T
        in_maps.append({
            "xall": x,
            "xown": np.ascontiguousarray(x[r * c.TOK:(r + 1) * c.TOK]),
            "wqkvT": np.ascontiguousarray(wqkvT),
            "wotT": np.ascontiguousarray(wotTm),
            "w1R": w1Rm, "w2R": w2Rm, "b1R": b1Rm, "b2R": b2_all,
            "centT": centTm, "biasr": biasm,
        })
    return in_maps


_CFG = Cfg()


def kernel(**inputs):
    c = _CFG
    in_maps = prep_inputs(c, inputs)
    nc = build(c)
    res = run_bass_kernel_spmd(nc, in_maps, core_ids=list(range(c.R)))
    out = np.concatenate([res.results[r]["y"] for r in range(c.R)], axis=0)
    return out.reshape(1, c.S, c.D).astype(np.float32)
